# version 02
# baseline (speedup 1.0000x reference)
import math
import numpy as np
from contextlib import ExitStack
import concourse.bass as bass
import concourse.mybir as mybir
from concourse.bass_utils import run_bass_kernel_spmd

F32 = mybir.dt.float32
F32R = mybir.dt.float32r
BF16 = mybir.dt.bfloat16
F16 = mybir.dt.float16
I32 = mybir.dt.int32
U8 = mybir.dt.uint8
AF = mybir.ActivationFunctionType
ALU = mybir.AluOpType

S, E = 4096, 4096
HL = 16
INTER = 11008
NG_H, NG_I = 32, 86
SC, NSC = 2048, 2
ROPE_BASE = 10000.0
ATT_SCALE = 1.0 / math.sqrt(128.0)
TWO_PI = 2.0 * math.pi
C1 = 6.28125
C2 = TWO_PI - C1
NCH = 43
ESZ = {BF16: 2, F16: 2, F32: 4, I32: 4, F32R: 4, U8: 1}


def build(debug=0, phases="ABCD"):
    nc = bass.Bass("TRN2", target_bir_lowering=False)

    x_in = nc.dram_tensor("x", [S, E], F32, kind="ExternalInput").ap()
    xown_in = nc.dram_tensor("xown", [2048, E], F32, kind="ExternalInput").ap()
    wqk_in = nc.dram_tensor("wqk", [32, E, 128], F32, kind="ExternalInput").ap()
    wv_in = nc.dram_tensor("wv", [E, 2048], F32, kind="ExternalInput").ap()
    bqk_in = nc.dram_tensor("bqk", [128, 32], F32, kind="ExternalInput").ap()
    bv_in = nc.dram_tensor("bv", [128, 16], F32, kind="ExternalInput").ap()
    wo_in = nc.dram_tensor("wo", [32, 32, 128, 128], F32, kind="ExternalInput").ap()
    bo_in = nc.dram_tensor("bo", [128, 32], F32, kind="ExternalInput").ap()
    gq_in = nc.dram_tensor("gq", [E // 8, INTER], I32, kind="ExternalInput").ap()
    gs_in = nc.dram_tensor("gsc", [NG_H, INTER], F16, kind="ExternalInput").ap()
    gz_in = nc.dram_tensor("gz", [NG_H, INTER // 8], I32, kind="ExternalInput").ap()
    uq_in = nc.dram_tensor("uq", [E // 8, INTER], I32, kind="ExternalInput").ap()
    us_in = nc.dram_tensor("usc", [NG_H, INTER], F16, kind="ExternalInput").ap()
    uz_in = nc.dram_tensor("uz", [NG_H, INTER // 8], I32, kind="ExternalInput").ap()
    dq_in = nc.dram_tensor("dq", [INTER // 8, E], I32, kind="ExternalInput").ap()
    ds_in = nc.dram_tensor("dsc", [NG_I, E], F16, kind="ExternalInput").ap()
    dz_in = nc.dram_tensor("dz", [NG_I, E // 8], I32, kind="ExternalInput").ap()
    eye_in = nc.dram_tensor("eye", [128, 128], F32, kind="ExternalInput").ap()
    mask_in = nc.dram_tensor("maskbig", [128, 1024], F32, kind="ExternalInput").ap()
    onesc_in = nc.dram_tensor("onescol", [128, 1], F32, kind="ExternalInput").ap()
    onesr_in = nc.dram_tensor("onesrow", [1, 128], F32, kind="ExternalInput").ap()
    shv_in = nc.dram_tensor("shiftvec", [128, 1], I32, kind="ExternalInput").ap()
    iota64_in = nc.dram_tensor("iota64", [1, 64], F32, kind="ExternalInput").ap()
    pos_in = nc.dram_tensor("pos", [1, S], F32, kind="ExternalInput").ap()
    oh86_in = nc.dram_tensor("oh86", [128, NG_I * NG_I], F32, kind="ExternalInput").ap()
    oh32_in = nc.dram_tensor("oh32", [128, NG_H * NG_H], F32, kind="ExternalInput").ap()
    rsel_in = nc.dram_tensor("rsel", [128, 2], F32, kind="ExternalInput").ap()

    out_t = nc.dram_tensor("out", [2048, E], F32, kind="ExternalOutput")

    kT_s = nc.dram_tensor("kT_s", [HL, 128, S], BF16)
    v_s = nc.dram_tensor("v_s", [HL, 128, 32, 128], BF16)
    q_s = nc.dram_tensor("q_s", [HL, 8, 128, 512], BF16)
    ag_in = [nc.dram_tensor(f"ag_in{p}", [HL * 4 * 128, 512], BF16) for p in range(2)]
    ag_out = [nc.dram_tensor(f"ag_out{p}", [2 * HL * 4 * 128, 512], BF16) for p in range(2)]
    wo_bf = nc.dram_tensor("wo_bf", [32, 32, 128, 128], BF16)
    xnew_s = nc.dram_tensor("xnew_s", [4, 128, 4 * E], F32)
    wg_sc = nc.dram_tensor("wg_sc", [NG_H, 128, INTER], BF16)
    wu_sc = nc.dram_tensor("wu_sc", [NG_H, 128, INTER], BF16)
    wd_sc = nc.dram_tensor("wd_sc", [32, 128, NG_I, 128], BF16)
    mz_g = nc.dram_tensor("mz_g", [NG_H, INTER], F32)
    mz_u = nc.dram_tensor("mz_u", [NG_H, INTER], F32)
    mz_d = nc.dram_tensor("mz_d", [NG_I, E], F32)
    s_gs = nc.dram_tensor("s_gs", [NG_H, INTER], F32)
    s_us = nc.dram_tensor("s_us", [NG_H, INTER], F32)
    s_ds = nc.dram_tensor("s_ds", [NG_I, E], F32)

    dbg = {}
    if debug:
        dbg["kT"] = nc.dram_tensor("dbg_kT", [HL, 128, S], BF16, kind="ExternalOutput")
        dbg["v"] = nc.dram_tensor("dbg_v", [HL, 128, 32, 128], BF16, kind="ExternalOutput")
        dbg["q"] = nc.dram_tensor("dbg_q", [HL, 8, 128, 512], BF16, kind="ExternalOutput")
        dbg["ag"] = nc.dram_tensor("dbg_ag", [2, 2 * HL * 4 * 128, 512], BF16,
                                   kind="ExternalOutput")
        dbg["xnew"] = nc.dram_tensor("dbg_xnew", [4, 128, 4 * E], F32, kind="ExternalOutput")
        dbg["wg"] = nc.dram_tensor("dbg_wg", [NG_H, 128, INTER], BF16, kind="ExternalOutput")
        dbg["mz"] = nc.dram_tensor("dbg_mz", [NG_H, INTER], F32, kind="ExternalOutput")
        dbg["xlnT_A"] = nc.dram_tensor("dbg_xlnT_A", [128, 65536], U8, kind="ExternalOutput")
        dbg["xlnT_B"] = nc.dram_tensor("dbg_xlnT_B", [128, 65536], U8, kind="ExternalOutput")

    es = ExitStack()
    T = lambda name, shape, dt_: es.enter_context(nc.sbuf_tensor(name, shape, dt_))

    bigA = T("bigA", [128, 65536], U8)
    bigB = T("bigB", [128, 65536], U8)
    arC = T("arC", [128, 49152], U8)
    arD = T("arD", [128, 16384], U8)

    def cv(t, off, n, dt_, p0=0, p1=128):
        return t[p0:p1, off:off + n * ESZ[dt_]].bitcast(dt_)

    idbf = T("idbf", [128, 128], BF16)
    idf16 = T("idf16", [128, 128], F16)
    idf32 = T("idf32", [128, 128], F32)
    onescol = T("onescol_t", [128, 1], F32)
    onesrow = T("onesrow_t", [1, 128], F32)
    shvec = T("shvec", [128, 1], I32)
    bqk_sb = T("bqk_sb", [128, 32], F32)
    bv_sb = T("bv_sb", [128, 16], F32)
    bo_sb = T("bo_sb", [128, 32], F32)
    epst = T("epst", [128, 1], F32)
    id32r = T("id32r", [32, 32], F32R)
    id86r = T("id86r", [NG_I, NG_I], F32R)
    rsel_sb = T("rsel_sb", [128, 2], F32)
    invf = T("invf", [1, 64], F32)
    iota64_sb = T("iota64_sb", [1, 64], F32)
    fifteen = T("fifteen", [128, 512], I32)
    xsum_sr = T("xsum_sr_t", [NG_H, 512], F32R)
    xsum2_sr = T("xsum2_sr_t", [NG_I, 512], F32R)
    ssr_one = T("ssr_one_t", [NG_I, 512], F32R)
    mzr0 = T("mzr0_t", [NG_I, 512], F32R)
    mzr1 = T("mzr1_t", [NG_H, 256], F32R)
    stats = T("stats", [128, 48], F32)
    mv = T("mv", [128, 2], F32)
    lnt1 = T("lnt1", [128, 1], F32)
    rstd = T("rstd", [128, 1], F32)

    pb = [es.enter_context(nc.psum_tensor(f"pb{i}", [128, 512], F32)) for i in range(6)]
    ptbf = es.enter_context(nc.psum_tensor("ptbf", [128, 1024], BF16))
    ptf16 = es.enter_context(nc.psum_tensor("ptf16", [128, 1024], F16))

    class Sem:
        __slots__ = ("h", "n")

        def __init__(self, name):
            self.h = es.enter_context(nc.semaphore(name))
            self.n = 0

    SM = {n: Sem(n) for n in ["cT", "cV", "cS", "cCC"]}
    cT, cV, cS, cCC = SM["cT"], SM["cV"], SM["cS"], SM["cCC"]
    ALL_ENG = ["tensor", "vector", "scalar", "sync", "gpsimd"]
    prog = {k: [] for k in ALL_ENG}

    def em(engname, fn):
        prog[engname].append(fn)

    class Stm:
        __slots__ = ("sem", "rel")

        def __init__(self, name):
            self.sem = Sem(name)
            self.rel = None

    streams = {}

    def stm(name):
        if name not in streams:
            streams[name] = Stm(name)
        return streams[name]

    def wait(engname, sem, val=None):
        v = sem.n if val is None else val

        def f(e, sem=sem, v=v):
            e.wait_ge(sem.h, v)
        em(engname, f)

    def dma(q, name, out, in_, **kw):
        st = stm(name)
        if st.rel is not None:
            rs, rv = st.rel
            wait(q, rs, rv)
            st.rel = None
        st.sem.n += 16

        def f(e, out=out, in_=in_, kw=kw, sem=st.sem):
            e.dma_start(out=out, in_=in_, **kw).then_inc(sem.h, 16)
        em(q, f)
        return (st.sem, st.sem.n)

    def srel(name, sem):
        stm(name).rel = (sem, sem.n)

    def swait(engname, name):
        st = stm(name)
        if st.sem.n > 0:
            wait(engname, st.sem, st.sem.n)

    def barrier(engs=None):
        engs = engs or ALL_ENG
        for eng in engs:
            for s in SM.values():
                if s.n > 0:
                    wait(eng, s, s.n)
            for st in streams.values():
                if st.sem.n > 0:
                    wait(eng, st.sem, st.sem.n)

    def t_mm(out, lhsT, rhs, start, stop, sem=None):
        if sem is not None:
            sem.n += 1

        def f(t, out=out, lhsT=lhsT, rhs=rhs, start=start, stop=stop, sem=sem):
            i = t.matmul(out, lhsT, rhs, start=start, stop=stop)
            if sem is not None:
                i.then_inc(sem.h, 1)
        em("tensor", f)

    def t_tp(out, in_, iden, sem=None):
        if sem is not None:
            sem.n += 1

        def f(t, out=out, in_=in_, iden=iden, sem=sem):
            i = t.transpose(out, in_, iden)
            if sem is not None:
                i.then_inc(sem.h, 1)
        em("tensor", f)

    ENG_SEM = {"vector": cV, "scalar": cS}

    def mk_op(engname):
        esem = ENG_SEM.get(engname)

        def op(name, *args, sem=None, **kw):
            if esem is not None:
                if esem.n > 0:
                    wait(engname, esem, esem.n)
                sem = esem
                sem.n += 1
            elif sem is not None:
                sem.n += 1

            def f(e, name=name, args=args, kw=kw, sem=sem):
                i = getattr(e, name)(*args, **kw)
                if sem is not None:
                    i.then_inc(sem.h, 1)
            em(engname, f)
        return op

    v_op, s_op, g_op = mk_op("vector"), mk_op("scalar"), mk_op("gpsimd")

    for dst, inp in [(idf32[:], eye_in), (onescol[:], onesc_in), (onesrow[:], onesr_in),
                     (shvec[:], shv_in), (bqk_sb[:], bqk_in), (bv_sb[:], bv_in),
                     (bo_sb[:], bo_in), (iota64_sb[:], iota64_in), (rsel_sb[:], rsel_in)]:
        dma("sync", "pro_c", dst, inp)
    dma("gpsimd", "pro_g", idbf[:], eye_in)
    dma("gpsimd", "pro_g", idf16[:], eye_in)
    for et in range(32):
        dma("gpsimd", "pro_w", wo_bf.ap()[et], wo_in[et], max_dma_last_dim=8192)
    dma("gpsimd", "pro_w", s_gs.ap(), gs_in, max_dma_last_dim=4096)
    dma("gpsimd", "pro_w", s_us.ap(), us_in, max_dma_last_dim=4096)
    dma("gpsimd", "pro_w", s_ds.ap(), ds_in, max_dma_last_dim=4096)
    swait("vector", "pro_c")
    swait("vector", "pro_g")
    v_op("memset", epst[:], 1e-5)
    v_op("memset", fifteen[:], 15)
    v_op("tensor_copy", id32r[:], idf32[0:32, 0:32])
    v_op("tensor_copy", id86r[:], idf32[0:NG_I, 0:NG_I], sem=cV)
    swait("scalar", "pro_c")
    s_op("activation", invf[:], iota64_sb[:], AF.Exp, scale=-math.log(ROPE_BASE) / 64.0, sem=cS)

    def emit_zprep(z_inp, mz_out, s_scr, ng, ncols):
        pk = ncols // 8
        zpak = cv(bigA, 0, pk, I32, 0, ng)
        zj = cv(bigA, pk * 4, pk, I32, 0, ng)
        ld0 = dma("sync", "pro_z", zpak, z_inp)
        nh = 2 if ncols * 8 > 65536 else 1
        hc = ncols // nh
        hpk = pk // nh
        for hf in range(nh):
            sall_h = cv(bigB, 0, hc, F32, 0, ng)
            mzsb_h = cv(bigB, hc * 4, hc, F32, 0, ng)
            if hf == 0:
                swait("sync", "pro_w")
            ld = dma("sync", "pro_z", sall_h, s_scr.ap()[:, hf * hc:(hf + 1) * hc])
            wait("vector", ld[0], ld[1])
            for j in range(8):
                v_op("tensor_scalar", zj[:, 0:hpk], zpak[:, hf * hpk:(hf + 1) * hpk],
                     4 * j, 15, ALU.logical_shift_right, ALU.bitwise_and)
                v_op("tensor_scalar",
                     mzsb_h.rearrange("g (m e) -> g m e", e=8)[:, :, j],
                     zj[:, 0:hpk], 1.0, -1.0, ALU.add, ALU.mult)
            v_op("tensor_tensor", mzsb_h, mzsb_h, sall_h, ALU.mult, sem=cV)
            wait("gpsimd", cV)
            dma("gpsimd", "pro_mz", mz_out.ap()[:, hf * hc:(hf + 1) * hc], mzsb_h)
            swait("vector", "pro_mz")
            srel("pro_z", cV)

    emit_zprep(gz_in, mz_g, s_gs, NG_H, INTER)
    emit_zprep(uz_in, mz_u, s_us, NG_H, INTER)
    emit_zprep(dz_in, mz_d, s_ds, NG_I, E)
    if debug:
        swait("sync", "pro_mz")
        dma("sync", "pro_c", dbg["mz"].ap(), mz_g.ap())
    barrier()

    def xlnT_tile(ech, col, n):
        t = bigA if ech < 16 else bigB
        return cv(t, (ech % 16) * SC * 2 + col * 2, n, BF16)

    def xlnT_8(e8, tt):
        t = bigA if e8 < 2 else bigB
        base = (e8 % 2) * 8 * SC * 2
        return cv(t, base, 8 * SC, BF16).rearrange(
            "p (ec t) -> p ec t", t=SC)[:, :, tt * 128:(tt + 1) * 128]

    x_f32 = [cv(arC, i * 16384, E, F32) for i in range(2)]
    xln_bf = cv(arC, 32768, E, BF16)
    wv_buf = [cv(arC, i * 16384, 32 * 256, BF16) for i in range(2)]
    v_out = [cv(arC, 32768 + i * 512, 256, BF16) for i in range(2)]
    wqk_buf = [cv(arC, i * 8192, 4096, BF16) for i in range(2)]
    rope_t1 = [cv(arC, 16384 + i * 2048, 512, F32) for i in range(2)]
    rope_t2 = cv(arC, 20480, 512, F32)
    rope_m1 = cv(arC, 22528, 512, F32)
    rope_m2 = cv(arC, 24576, 512, F32)
    qk_out = [cv(arC, 26624 + i * 1024, 512, BF16) for i in range(2)]
    cs_c = cv(arC, 40960, SC, BF16)
    cs_s = cv(arC, 45056, SC, BF16)
    tbl_t = [cv(arD, i * 2048, 512, F32, 0, 64) for i in range(4)]
    tbl_i = cv(arD, 8192, 512, I32, 0, 64)
    posb = cv(arD, 10240, 512, F32, 0, 1)

    def emit_tables(sc, brel):
        for sl in range(4):
            ld = dma("sync", "a_pos", posb,
                     pos_in[:, sc * SC + sl * 512: sc * SC + (sl + 1) * 512])
            wait("tensor", ld[0], ld[1])
            rs, rv = brel[4]
            wait("tensor", rs, rv)
            t_mm(pb[4][0:64, :], invf[:], posb, True, True, sem=cT)
            srel("a_pos", cT)
            wait("vector", cT)
            ang = pb[4][0:64, :]
            v_op("tensor_scalar", tbl_i, ang, 1.0 / TWO_PI, None, ALU.mult, ALU.bypass)
            v_op("tensor_copy", tbl_t[0], tbl_i)
            v_op("tensor_scalar", tbl_t[1], tbl_t[0], C1, None, ALU.mult, ALU.bypass)
            v_op("tensor_tensor", tbl_t[1], ang, tbl_t[1], ALU.subtract)
            v_op("tensor_scalar", tbl_t[2], tbl_t[0], C2, None, ALU.mult, ALU.bypass)
            v_op("tensor_tensor", tbl_t[1], tbl_t[1], tbl_t[2], ALU.subtract, sem=cV)
            brel[4] = (cV, cV.n)
            wait("scalar", cV)
            s_op("activation", tbl_t[2], tbl_t[1], AF.Sin, scale=0.5)
            s_op("activation", tbl_t[3], tbl_t[1], AF.Sin, scale=0.25, sem=cS)
            wait("vector", cS)
            sl_c = cs_c[0:64, sl * 512:(sl + 1) * 512]
            sl_s = cs_s[64:128, sl * 512:(sl + 1) * 512]
            v_op("tensor_tensor", tbl_t[3], tbl_t[3], tbl_t[3], ALU.mult)
            v_op("tensor_scalar", tbl_t[3], tbl_t[3], -2.0, 1.0, ALU.mult, ALU.add)
            v_op("tensor_tensor", tbl_t[0], tbl_t[2], tbl_t[2], ALU.mult)
            v_op("tensor_scalar", sl_c, tbl_t[0], -2.0, 1.0, ALU.mult, ALU.add)
            v_op("tensor_tensor", tbl_t[2], tbl_t[2], tbl_t[3], ALU.mult)
            v_op("tensor_scalar", sl_s, tbl_t[2], 2.0, None, ALU.mult, ALU.bypass)
            v_op("tensor_copy", cs_c[64:128, sl * 512:(sl + 1) * 512], sl_c)
            v_op("tensor_scalar", cs_s[0:64, sl * 512:(sl + 1) * 512], sl_s, -1.0, None,
                 ALU.mult, ALU.bypass, sem=cV)
        return cV.n

    def emit_phaseA():
        brel = {i: (cS, 0) for i in range(6)}
        for sc in range(NSC):
            for tt in range(16):
                ttg = sc * 16 + tt
                xb = x_f32[tt % 2]
                ld = dma("sync", f"a_x{tt % 2}", xb, x_in[ttg * 128:(ttg + 1) * 128, :])
                wait("vector", ld[0], ld[1])
                for i in range(8):
                    v_op("bn_stats", stats[:, i * 6:(i + 1) * 6], xb[:, i * 512:(i + 1) * 512])
                v_op("bn_aggr", mv[:], stats[:], sem=cV)
                wait("scalar", cV)
                s_op("activation", lnt1[:], mv[:, 1:2], AF.Ln, bias=epst[:])
                s_op("activation", rstd[:], lnt1[:], AF.Exp, scale=-0.5, sem=cS)
                wait("vector", cS)
                v_op("tensor_scalar", xln_bf, xb, mv[:, 0:1], rstd[:],
                     ALU.subtract, ALU.mult, sem=cV)
                srel(f"a_x{tt % 2}", cV)
                for e8 in range(4):
                    wait("tensor", cV)
                    for j in range(8):
                        c = e8 * 8 + j
                        t_tp(ptbf[:, j * 128:(j + 1) * 128], xln_bf[:, c * 128:(c + 1) * 128],
                             idbf[:], sem=(cT if j == 7 else None))
                    wait("vector", cT)
                    v_op("tensor_copy", xlnT_8(e8, tt),
                         ptbf[:].rearrange("p (a b) -> p a b", b=128), sem=cV)
            xlnT_V = cV.n
            if debug and sc == 0:
                wait("sync", cV, xlnT_V)
                dma("sync", "pro_c", dbg["xlnT_A"].ap(), bigA[:, :])
                dma("sync", "pro_c", dbg["xlnT_B"].ap(), bigB[:, :])
                swait("gpsimd", "pro_c")
                swait("vector", "pro_c")
                swait("tensor", "pro_c")
            tbl_V = emit_tables(sc, brel)
            for vb in range(8):
                wb = wv_buf[vb % 2]
                if vb < 2:
                    stm(f"a_wv{vb % 2}").rel = (cV, xlnT_V)
                ld = dma("gpsimd", f"a_wv{vb % 2}", wb,
                         wv_in.rearrange("(ec p) f -> p ec f", p=128)
                         [:, :, vb * 256:(vb + 1) * 256])
                wait("tensor", ld[0], ld[1])
                wait("tensor", cV, xlnT_V)
                for tt in range(16):
                    ttg = sc * 16 + tt
                    bank = 4 + (tt % 2)
                    rsb, rvb = brel[bank]
                    wait("tensor", rsb, rvb)
                    ps = pb[bank][:, 0:256]
                    for ech in range(32):
                        t_mm(ps, xlnT_tile(ech, tt * 128, 128), wb[:, ech * 256:(ech + 1) * 256],
                             ech == 0, ech == 31, sem=(cT if ech == 31 else None))
                    wait("scalar", cT)
                    vo = v_out[tt % 2]
                    swait("scalar", f"a_vo{tt % 2}")
                    s_op("copy", vo, ps, sem=cS)
                    brel[bank] = (cS, cS.n)
                    wait("gpsimd", cS)
                    dma("gpsimd", f"a_vo{tt % 2}",
                        v_s.ap()[vb * 2:vb * 2 + 2, :, ttg, :].rearrange("h p d -> p h d"),
                        vo.rearrange("p (h d) -> p h d", d=128))
                srel(f"a_wv{vb % 2}", cT)
            for ft in range(32):
                wb = wqk_buf[ft % 2]
                if ft < 2:
                    stm(f"a_wqk{ft % 2}").rel = (cT, cT.n)
                ld = dma("gpsimd", f"a_wqk{ft % 2}", wb,
                         wqk_in[ft].rearrange("(ec p) f -> p ec f", p=128))
                wait("tensor", ld[0], ld[1])
                for tc in range(4):
                    bank = tc
                    rsb, rvb = brel[bank]
                    wait("tensor", rsb, rvb)
                    ps = pb[bank][:]
                    for ech in range(32):
                        t_mm(ps, wb[:, ech * 128:(ech + 1) * 128], xlnT_tile(ech, tc * 512, 512),
                             ech == 0, ech == 31, sem=(cT if ech == 31 else None))
                    wait("scalar", cT)
                    t1 = rope_t1[tc % 2]
                    s_op("activation", t1, ps, AF.Identity, bias=bqk_sb[:, ft:ft + 1],
                         scale=1.0, sem=cS)
                    brel[bank] = (cS, cS.n)
                    wait("vector", cS)
                    if ft == 0 and tc == 0:
                        wait("vector", cV, tbl_V)
                    v_op("tensor_copy", rope_t2[0:64, :], t1[64:128, :])
                    v_op("tensor_copy", rope_t2[64:128, :], t1[0:64, :])
                    csl = slice(tc * 512, (tc + 1) * 512)
                    v_op("tensor_tensor", rope_m1, t1, cs_c[:, csl], ALU.mult, sem=cV)
                    v_op("tensor_tensor", rope_m2, rope_t2, cs_s[:, csl], ALU.mult)
                    qo = qk_out[tc % 2]
                    swait("vector", f"a_qo{tc % 2}")
                    v_op("tensor_tensor", qo, rope_m1, rope_m2, ALU.add, sem=cV)
                    wait("gpsimd", cV)
                    if ft < 16:
                        dma("gpsimd", f"a_qo{tc % 2}", q_s.ap()[ft, sc * 4 + tc], qo)
                    else:
                        dma("gpsimd", f"a_qo{tc % 2}",
                            kT_s.ap()[ft - 16][:, sc * SC + tc * 512: sc * SC + (tc + 1) * 512],
                            qo)
                srel(f"a_wqk{ft % 2}", cT)
            barrier()
        if debug:
            for h in range(HL):
                dma("sync", "pro_c", dbg["kT"].ap()[h], kT_s.ap()[h])
                dma("sync", "pro_c", dbg["q"].ap()[h], q_s.ap()[h])
                dma("sync", "pro_c", dbg["v"].ap()[h], v_s.ap()[h])

    def emit_phaseB():
        kv_k = [cv(arC, i * 8192, S, BF16) for i in range(2)]
        kv_v = [cv(arC, 16384 + i * 8192, S, BF16) for i in range(2)]
        q_buf = [cv(arC, 32768 + i * 1024, 512, BF16) for i in range(2)]
        probs = [cv(arC, 34816 + i * 1024, 512, BF16) for i in range(2)]
        ctx_sb = [cv(arC, 36864 + i * 1024, 512, BF16) for i in range(2)]
        acc_sb = cv(arC, 38912, 512, F32)
        rb_sb = cv(arC, 40960, 512, F32)
        rec_sb = cv(arC, 43008, 512, F32, 0, 1)
        maskbf = cv(arC, 45056, 1024, BF16)
        barrier()
        mld = dma("gpsimd", "b_msk", maskbf, mask_in)
        wait("vector", mld[0], mld[1])
        pv_rel = {2: (cV, 0), 3: (cV, 0)}
        sc_rel = {0: (cV, 0), 1: (cV, 0)}
        for pss in range(2):
            for h in range(HL):
                ib = h % 2
                dma("sync", f"b_kv{ib}", kv_k[ib], kT_s.ap()[h])
                kvt = dma("sync", f"b_kv{ib}", kv_v[ib],
                          v_s.ap()[h].rearrange("p tt d -> p (tt d)"))
                for qq in range(4):
                    qc = pss * 4 + qq
                    qb = q_buf[qc % 2]
                    qt = dma("sync", f"b_q{qc % 2}", qb, q_s.ap()[h, qc])
                    nkt = 4 * qc + 4
                    wait("tensor", qt[0], qt[1])
                    wait("tensor", kvt[0], kvt[1])
                    for kt in range(nkt):
                        sb_ = kt % 2
                        rs2, rv2 = sc_rel[sb_]
                        wait("tensor", rs2, rv2)
                        t_mm(pb[sb_][:], kv_k[ib][:, kt * 128:(kt + 1) * 128], qb, True, True,
                             sem=cT)
                        wait("scalar", cT)
                        pr = probs[sb_]
                        s_op("activation", pr, pb[sb_][:], AF.Exp, scale=ATT_SCALE, sem=cS)
                        sc_rel[sb_] = (cS, cS.n)
                        wait("vector", cS)
                        r = kt - 4 * qc
                        if r >= 0:
                            v_op("tensor_tensor", pr, pr,
                                 maskbf[:, 512 - 128 * r: 1024 - 128 * r], ALU.mult)
                        if kt == 0:
                            v_op("tensor_copy", pb[4][:], pr, sem=cV)
                        else:
                            v_op("tensor_tensor", pb[4][:], pr, pb[4][:], ALU.add, sem=cV)
                        cbank = 2 + (qc % 2)
                        if kt == 0:
                            rs4, rv4 = pv_rel[cbank]
                            wait("tensor", rs4, rv4)
                        wait("tensor", cV)
                        t_mm(pb[cbank][:], kv_v[ib][:, kt * 128:(kt + 1) * 128], pr,
                             kt == 0, kt == nkt - 1, sem=(cT if kt == nkt - 1 else None))
                    wait("vector", cT)
                    v_op("tensor_copy", acc_sb, pb[4][:], sem=cV)
                    wait("tensor", cV)
                    t_mm(pb[5][0:1, :], onescol[:], acc_sb, True, True, sem=cT)
                    wait("vector", cT)
                    v_op("reciprocal", rec_sb, pb[5][0:1, :], sem=cV)
                    wait("tensor", cV)
                    t_mm(pb[5][:], onesrow[:], rec_sb, True, True, sem=cT)
                    wait("vector", cT)
                    v_op("tensor_copy", rb_sb, pb[5][:], sem=cV)
                    cx = ctx_sb[qc % 2]
                    swait("vector", f"b_cx{qc % 2}")
                    v_op("tensor_tensor", rb_sb, pb[2 + (qc % 2)][:], rb_sb, ALU.mult)
                    v_op("tensor_scalar", cx, rb_sb, bv_sb[:, h:h + 1], None,
                         ALU.add, ALU.bypass, sem=cV)
                    pv_rel[2 + (qc % 2)] = (cV, cV.n)
                    wait("gpsimd", cV)
                    dma("gpsimd", f"b_cx{qc % 2}",
                        ag_in[pss].ap().rearrange("(h q p) t -> h q p t", q=4, p=128)[h, qq],
                        cx)
                    srel(f"b_q{qc % 2}", cT)
                srel(f"b_kv{ib}", cT)
            swait("gpsimd", "b_cx0")
            swait("gpsimd", "b_cx1")
            cCC.n += 1

            def fcc(g, pss_=pss):
                g.collective_compute(
                    "AllGather", ALU.bypass,
                    replica_groups=[[0, 1], [2, 3], [4, 5], [6, 7]],
                    ins=[ag_in[pss_].ap().opt()], outs=[ag_out[pss_].ap().opt()],
                ).then_inc(cCC.h, 1)
            em("gpsimd", fcc)
        if debug:
            wait("sync", cCC, 2)
            for p in range(2):
                dma("sync", "pro_c", dbg["ag"].ap()[p], ag_out[p].ap())

    def emit_phaseCD():
        ctx_all = [cv(bigA, i * 32768, 32 * 512, BF16) for i in range(2)]
        xnew = cv(bigB, 0, 4 * E, F32)
        x2T = cv(bigA, 0, 32 * 512, BF16)
        x2bf = cv(bigA, 32768, E, BF16)
        oh86_sb = cv(bigA, 0, NG_I * NG_I, BF16)
        oh32_sb = cv(bigA, 63488, NG_H * NG_H, BF16)
        stg_f = cv(bigA, 36864, 512, F32, 0, NG_I)
        wo_buf = [cv(arC, i * 8192, 32 * 128, BF16) for i in range(2)]
        to_sb = [cv(arC, 16384 + i * 2048, 512, F32) for i in range(2)]
        xn_slC = [cv(arC, 20480 + i * 2048, 512, F32) for i in range(2)]
        wgu = [cv(arC, 0, NG_H * 256, BF16), cv(arC, 16384, NG_H * 256, BF16),
               cv(arC, 32768, NG_H * 256, BF16)]
        wd_buf = [cv(arC, i * 22016, NG_I * 128, BF16) for i in range(2)]
        xn_slD = [cv(arC, 44032 + i * 2048, 512, F32) for i in range(2)]
        pk_buf = [cv(arD, i * 2048, 512, I32) for i in range(2)]
        nib_buf = cv(arD, 4096, 512, I32)
        wd_dq = cv(arD, 6144, 512, BF16)
        silu_sb = [cv(arD, 8192 + i * 2048, 512, F32) for i in range(2)]
        o_sb = [cv(arD, 8192 + i * 2048, 512, F32) for i in range(2)]
        d16_sb = [cv(arD, 12288 + i * 1024, 512, F16) for i in range(2)]

        def interT_tile(it):
            if it < 64:
                return cv(bigB, it * 1024, 512, BF16)
            return cv(bigA, 40960 + (it - 64) * 1024, 512, BF16)

        barrier()
        wait("sync", cCC, 2)
        wait("gpsimd", cCC, 2)
        for tci in range(4):
            for blk in range(2):
                dma("sync", f"c_ctx{blk}", ctx_all[blk],
                    ag_out[blk].ap().rearrange("(r h q p) t -> r h q p t", h=HL, q=4, p=128)
                    [:, :, tci].rearrange("r h p t -> p r h t"))
            swait("vector", "c_ctx0")
            swait("vector", "c_ctx1")
            v_op("tensor_scalar", ctx_all[1], ctx_all[1], rsel_sb[:, 1:2], None,
                 ALU.mult, ALU.bypass)
            v_op("tensor_scalar", ctx_all[0], ctx_all[0], rsel_sb[:, 0:1], None,
                 ALU.mult, ALU.bypass)
            v_op("tensor_tensor", ctx_all[0], ctx_all[0], ctx_all[1], ALU.add, sem=cV)
            ctx_V = cV.n
            srel("c_ctx1", cV)
            stm("d_oh32").rel = (cV, ctx_V)
            o32 = dma("gpsimd", "d_oh32", oh32_sb, oh32_in)
            o_psum_rel = {0: (cS, 0), 1: (cS, 0)}
            tp_rel = (cV, 0)
            for et in range(32):
                wb = wo_buf[et % 2]
                wl = dma("gpsimd", f"c_wo{et % 2}", wb,
                         wo_bf.ap()[et].rearrange("ct p f -> p ct f"))
                xl = dma("sync", f"c_xo{et % 2}", xn_slC[et % 2],
                         xown_in.rearrange("(tc j p) e -> tc j p e", j=4, p=128)
                         [tci, :, :, et * 128:(et + 1) * 128].rearrange("j p e -> p j e"))
                wait("tensor", wl[0], wl[1])
                wait("tensor", cV, ctx_V)
                bank = et % 2
                rs2, rv2 = o_psum_rel[bank]
                wait("tensor", rs2, rv2)
                for ct in range(32):
                    t_mm(pb[bank][:], wb[:, ct * 128:(ct + 1) * 128],
                         ctx_all[0][:, ct * 512:(ct + 1) * 512],
                         ct == 0, ct == 31, sem=(cT if ct == 31 else None))
                srel(f"c_wo{et % 2}", cT)
                wait("scalar", cT)
                swait_to = to_sb[et % 2]
                s_op("activation", swait_to, pb[bank][:], AF.Identity,
                     bias=bo_sb[:, et:et + 1], scale=1.0, sem=cS)
                o_psum_rel[bank] = (cS, cS.n)
                wait("tensor", cS)
                rs4, rv4 = tp_rel
                wait("tensor", rs4, rv4)
                for j in range(4):
                    t_tp(pb[2][:, j * 128:(j + 1) * 128],
                         to_sb[et % 2][:, j * 128:(j + 1) * 128],
                         idf32[:], sem=(cT if j == 3 else None))
                wait("vector", cT)
                wait("vector", xl[0], xl[1])
                v_op("tensor_tensor",
                     xnew.rearrange("p (j e) -> p j e", e=E)[:, :, et * 128:(et + 1) * 128],
                     pb[2][:].rearrange("p (j e) -> p j e", e=128),
                     xn_slC[et % 2].rearrange("p (j e) -> p j e", e=128),
                     ALU.add, sem=cV)
                tp_rel = (cV, cV.n)
                srel(f"c_xo{et % 2}", cV)
            ctx_done_T = cT.n
            srel("c_ctx0", cT)
            for j in range(4):
                xnj = xnew.rearrange("p (j e) -> p j e", e=E)[:, j, :]
                wait("vector", cV)
                for i in range(8):
                    v_op("bn_stats", stats[:, i * 6:(i + 1) * 6], xnj[:, i * 512:(i + 1) * 512])
                v_op("bn_aggr", mv[:], stats[:], sem=cV)
                wait("scalar", cV)
                s_op("activation", lnt1[:], mv[:, 1:2], AF.Ln, bias=epst[:])
                s_op("activation", rstd[:], lnt1[:], AF.Exp, scale=-0.5, sem=cS)
                wait("vector", cS)
                wait("vector", cT, ctx_done_T)
                v_op("tensor_scalar", x2bf, xnj, mv[:, 0:1], rstd[:],
                     ALU.subtract, ALU.mult, sem=cV)
                for e8 in range(4):
                    wait("tensor", cV)
                    for jj in range(8):
                        c = e8 * 8 + jj
                        t_tp(ptbf[:, jj * 128:(jj + 1) * 128], x2bf[:, c * 128:(c + 1) * 128],
                             idbf[:], sem=(cT if jj == 7 else None))
                    wait("vector", cT)
                    v_op("tensor_copy",
                         x2T.rearrange("p (ec t) -> p ec t", t=512)
                         [:, e8 * 8:e8 * 8 + 8, j * 128:(j + 1) * 128],
                         ptbf[:].rearrange("p (a b) -> p a b", b=128), sem=cV)
            x2T_V = cV.n
            wait("gpsimd", cV, x2T_V)
            spl = dma("gpsimd", "x_spill", xnew_s.ap()[tci], xnew)
            if debug:
                dma("gpsimd", "x_spill", dbg["xnew"].ap()[tci], xnew)
            wait("tensor", cV, x2T_V)
            wait("tensor", o32[0], o32[1])
            for g in range(NG_H):
                t_mm(pb[5][0:NG_H, :], oh32_sb[:, g * NG_H:(g + 1) * NG_H],
                     x2T[:, g * 512:(g + 1) * 512], g == 0, g == NG_H - 1,
                     sem=(cT if g == NG_H - 1 else None))
            srel("d_oh32", cT)
            wait("vector", cT)
            v_op("tensor_copy", xsum_sr[:], pb[5][0:NG_H, :], sem=cV)
            xsum_V = cV.n
            swait("tensor", "x_spill")
            swait("vector", "x_spill")
            swait("sync", "x_spill")
            swait("gpsimd", "x_spill")
            gu_rel = {0: (cS, 0), 1: (cS, 0), 2: (cV, 0), 3: (cV, 0)}
            sb_rel = {4: (cV, 0), 5: (cV, 0)}
            itg = 0
            for ci in range(NCH):
                coff = ci * 256
                bufs = []
                for mi, (mat_pk, mat_s, mat_mz, mat_sc) in enumerate(
                        [(gq_in, s_gs, mz_g, wg_sc), (uq_in, s_us, mz_u, wu_sc)]):
                    bname = f"d_wg{ci % 2}" if mi == 0 else "d_wu"
                    wb = wgu[(ci % 2) if mi == 0 else 2]
                    if tci == 0:
                        sld = dma("sync", "d_s", stg_f[0:NG_H, 0:256],
                                  mat_s.ap()[:, coff:coff + 256])
                        wait("vector", sld[0], sld[1])
                        v_op("tensor_copy", ssr_one[0:NG_H, mi * 256:(mi + 1) * 256],
                             stg_f[0:NG_H, 0:256], sem=cV)
                        srel("d_s", cV)
                        srdy = cV.n
                        st_ = stm(bname)
                        if st_.rel is not None:
                            wait("vector", st_.rel[0], st_.rel[1])
                            st_.rel = None
                        for g in range(NG_H):
                            pld = dma("sync", f"d_pk{g % 2}", pk_buf[g % 2][:, 0:256],
                                      mat_pk[g * 16:(g + 1) * 16, coff:coff + 256]
                                      .rearrange("r (o c) -> r o c", o=1)
                                      .to_broadcast([16, 8, 256]))
                            wait("vector", pld[0], pld[1])
                            v_op("scalar_tensor_tensor", nib_buf[:, 0:256],
                                 pk_buf[g % 2][:, 0:256], shvec[:], fifteen[:, 0:256],
                                 ALU.logical_shift_right, ALU.bitwise_and)
                            srel(f"d_pk{g % 2}", cV)
                            bank = 4 + (g % 2)
                            rsb, rvb = sb_rel[bank]
                            wait("tensor", rsb, rvb)
                            wait("tensor", cV, srdy)
                            t_mm(pb[bank][:, 0:256],
                                 id32r[:, g:g + 1].to_broadcast([NG_H, 128]),
                                 ssr_one[0:NG_H, mi * 256:(mi + 1) * 256], True, True, sem=cT)
                            wait("vector", cT)
                            v_op("tensor_tensor", wb[:, g * 256:(g + 1) * 256],
                                 nib_buf[:, 0:256], pb[bank][:, 0:256], ALU.mult, sem=cV)
                            sb_rel[bank] = (cV, cV.n)
                        wdone = cV.n
                        wait("gpsimd", cV, wdone)
                        dma("gpsimd", "d_wst", mat_sc.ap()[:, :, coff:coff + 256]
                            .rearrange("g p c -> p g c"), wb)
                        bufs.append((wb, (cV, wdone), bname))
                    else:
                        wld = dma("sync", bname, wb,
                                  mat_sc.ap()[:, :, coff:coff + 256]
                                  .rearrange("g p c -> p g c"))
                        bufs.append((wb, wld, bname))
                for mi, mat_mz in enumerate([mz_g, mz_u]):
                    mld = dma("sync", "d_s", stg_f[0:NG_H, 0:256],
                              mat_mz.ap()[:, coff:coff + 256])
                    wait("vector", mld[0], mld[1])
                    mzdst = mzr0[0:NG_H, 0:256] if mi == 0 else mzr1[0:NG_H, 0:256]
                    swait("vector", f"d_mzg{mi}")
                    v_op("tensor_copy", mzdst, stg_f[0:NG_H, 0:256], sem=cV)
                    srel("d_s", cV)
                mz_V = cV.n
                for it in range(2):
                    for mi in range(2):
                        wb, wrel, bname = bufs[mi]
                        bank = (0 + itg % 2) if mi == 0 else (2 + itg % 2)
                        rs2, rv2 = gu_rel[bank]
                        wait("tensor", rs2, rv2)
                        wait("tensor", wrel[0], wrel[1])
                        wait("tensor", cV, mz_V)
                        wait("tensor", cV, xsum_V)
                        for g in range(NG_H):
                            t_mm(pb[bank][:], wb[:, g * 256 + it * 128: g * 256 + it * 128 + 128],
                                 x2T[:, g * 512:(g + 1) * 512], g == 0, False)
                        mzsrc = mzr0 if mi == 0 else mzr1
                        t_mm(pb[bank][:], mzsrc[0:NG_H, it * 128:it * 128 + 128],
                             xsum_sr[0:NG_H, :], False, True, sem=cT)
                    wait("scalar", cT)
                    sl = silu_sb[itg % 2]
                    s_op("activation", sl, pb[0 + itg % 2][:], AF.Silu, sem=cS)
                    gu_rel[0 + itg % 2] = (cS, cS.n)
                    wait("vector", cS)
                    v_op("tensor_tensor", interT_tile(itg), sl, pb[2 + itg % 2][:], ALU.mult,
                         sem=cV)
                    gu_rel[2 + itg % 2] = (cV, cV.n)
                    itg += 1
                for _, _, bname in bufs:
                    srel(bname, cT)
            inter_V = cV.n
            wait("gpsimd", cT)
            ol = dma("gpsimd", "d_oh86", oh86_sb, oh86_in)
            wait("tensor", ol[0], ol[1])
            wait("tensor", cV, inter_V)
            for g in range(NG_I):
                t_mm(pb[5][0:NG_I, :], oh86_sb[:, g * NG_I:(g + 1) * NG_I],
                     interT_tile(g), g == 0, g == NG_I - 1, sem=(cT if g == NG_I - 1 else None))
            srel("d_oh86", cT)
            wait("vector", cT)
            v_op("tensor_copy", xsum2_sr[:], pb[5][0:NG_I, :], sem=cV)
            xsum2_V = cV.n
            barrier(["sync", "gpsimd", "vector"])
            d_rel = {0: (cV, 0), 1: (cV, 0)}
            mzd_V = 0
            for ec in range(8):
                if tci == 0:
                    sld = dma("sync", "d_s", stg_f[0:NG_I, 0:512],
                              s_ds.ap()[:, ec * 512:(ec + 1) * 512])
                    wait("vector", sld[0], sld[1])
                    v_op("tensor_copy", ssr_one[0:NG_I, 0:512],
                         stg_f[0:NG_I, 0:512], sem=cV)
                    srel("d_s", cV)
                    srdy2 = cV.n
                    for g in range(NG_I):
                        pld = dma("sync", f"d_pk{g % 2}", pk_buf[g % 2],
                                  dq_in[g * 16:(g + 1) * 16, ec * 512:(ec + 1) * 512]
                                  .rearrange("r (o c) -> r o c", o=1)
                                  .to_broadcast([16, 8, 512]))
                        wait("vector", pld[0], pld[1])
                        v_op("scalar_tensor_tensor", nib_buf[:, 0:512], pk_buf[g % 2],
                             shvec[:], fifteen[:], ALU.logical_shift_right, ALU.bitwise_and)
                        srel(f"d_pk{g % 2}", cV)
                        bank = 4 + (g % 2)
                        rsb, rvb = sb_rel[bank]
                        wait("tensor", rsb, rvb)
                        wait("tensor", cV, srdy2)
                        t_mm(pb[bank][:], id86r[:, g:g + 1].to_broadcast([NG_I, 128]),
                             ssr_one[0:NG_I, 0:512], True, True, sem=cT)
                        wait("vector", cT)
                        swait("vector", "d_wst")
                        v_op("tensor_tensor", wd_dq, nib_buf[:, 0:512], pb[bank][:],
                             ALU.mult, sem=cV)
                        sb_rel[bank] = (cV, cV.n)
                        wait("gpsimd", cV)
                        dma("gpsimd", "d_wst",
                            wd_sc.ap()[4 * ec:4 * ec + 4, :, g, :]
                            .rearrange("e4 p d -> p e4 d"), wd_dq)
                    swait("sync", "d_wst")
                for et4 in range(4):
                    et = ec * 4 + et4
                    bi = et % 2
                    wld = dma("sync", f"d_wd{bi}", wd_buf[bi],
                              wd_sc.ap()[et].rearrange("p g d -> p (g d)"))
                    wait("sync", cV, mzd_V)
                    mld = dma("sync", "d_s", stg_f[0:NG_I, 0:128],
                              mz_d.ap()[:, et * 128:(et + 1) * 128])
                    wait("vector", mld[0], mld[1])
                    v_op("tensor_copy", mzr0[0:NG_I, 0:128],
                         stg_f[0:NG_I, 0:128], sem=cV)
                    srel("d_s", cV)
                    mzd_V = cV.n
                    xld = dma("sync", f"d_xn{et % 2}", xn_slD[et % 2],
                              xnew_s.ap()[tci].rearrange("p (j e) -> p j e", e=E)
                              [:, :, et * 128:(et + 1) * 128])
                    bank = et % 2
                    rs2, rv2 = d_rel[bank]
                    wait("tensor", rs2, rv2)
                    wait("tensor", wld[0], wld[1])
                    wait("tensor", cV, mzd_V)
                    wait("tensor", cV, xsum2_V)
                    for g in range(NG_I):
                        t_mm(pb[bank][:], wd_buf[bi][:, g * 128:(g + 1) * 128],
                             interT_tile(g), g == 0, False)
                    t_mm(pb[bank][:], mzr0[0:NG_I, 0:128], xsum2_sr[0:NG_I, :],
                         False, True, sem=cT)
                    srel(f"d_wd{bi}", cT)
                    wait("vector", cT)
                    d16 = d16_sb[et % 2]
                    v_op("tensor_copy", d16, pb[bank][:], sem=cV)
                    d_rel[bank] = (cV, cV.n)
                    wait("tensor", cV)
                    for j in range(4):
                        t_tp(ptf16[:, j * 128:(j + 1) * 128], d16[:, j * 128:(j + 1) * 128],
                             idf16[:], sem=(cT if j == 3 else None))
                    wait("vector", cT)
                    wait("vector", xld[0], xld[1])
                    ob = o_sb[et % 2]
                    swait("vector", f"d_out{et % 2}")
                    v_op("tensor_tensor", ob, ptf16[:, 0:512], xn_slD[et % 2], ALU.add, sem=cV)
                    srel(f"d_xn{et % 2}", cV)
                    wait("gpsimd", cV)
                    dma("gpsimd", f"d_out{et % 2}",
                        out_t.ap().rearrange("(tc j p) e -> tc j p e", j=4, p=128)
                        [tci, :, :, et * 128:(et + 1) * 128].rearrange("j p e -> p j e"),
                        ob)
            barrier()
        swait("gpsimd", "d_out0")
        swait("gpsimd", "d_out1")

    emit_phaseA()
    if "B" in phases:
        emit_phaseB()
    if "C" in phases:
        emit_phaseCD()
    else:
        swait("gpsimd", "pro_c")

    block = es.enter_context(nc.Block())

    @block.sync
    def _(e):
        for f in prog["sync"]:
            f(e)

    @block.gpsimd
    def _(e):
        for f in prog["gpsimd"]:
            f(e)

    @block.tensor
    def _(e):
        for f in prog["tensor"]:
            f(e)

    @block.vector
    def _(e):
        for f in prog["vector"]:
            f(e)

    @block.scalar
    def _(e):
        for f in prog["scalar"]:
            f(e)

    es.close()
    return nc


def make_inputs(core, full):
    E_ = 4096
    r = core % 2
    b = core // 2
    x = np.ascontiguousarray(np.asarray(full["L0_attn_hidden_states"])[b])
    wqkv = np.asarray(full["L0_attn_qkv_w"])
    qkvb = np.asarray(full["L0_attn_qkv_b"])
    hs = slice(16 * r * 128, 16 * (r + 1) * 128)
    wq = wqkv[:, 0:4096][:, hs].reshape(E_, 16, 128)
    wk = wqkv[:, 4096:8192][:, hs].reshape(E_, 16, 128)
    wqk = np.ascontiguousarray(np.concatenate([wq, wk], axis=1).transpose(1, 0, 2))
    wv = np.ascontiguousarray(wqkv[:, 8192:12288][:, hs])
    bq = qkvb[0:4096][hs].reshape(16, 128)
    bk = qkvb[4096:8192][hs].reshape(16, 128)
    bv = qkvb[8192:12288][hs].reshape(16, 128)
    bqk = np.ascontiguousarray(np.concatenate([bq, bk], axis=0).T)
    wo = np.asarray(full["L0_attn_out_w"])
    wo_r = np.ascontiguousarray(wo.reshape(32, 128, 32, 128).transpose(2, 0, 1, 3))
    bo = np.ascontiguousarray(np.asarray(full["L0_attn_out_b"]).reshape(32, 128).T)
    mask = np.zeros((128, 1024), np.float32)
    for i in range(128):
        mask[i, 512 + i:] = 1.0
    oh32 = np.zeros((128, NG_H * NG_H), np.float32)
    for g in range(NG_H):
        oh32[:, g * NG_H + g] = 1.0
    oh86 = np.zeros((128, NG_I * NG_I), np.float32)
    for g in range(NG_I):
        oh86[:, g * NG_I + g] = 1.0
    rsel = np.zeros((128, 2), np.float32)
    rsel[:, 0] = 1.0 - r
    rsel[:, 1] = r
    return {
        "x": x,
        "xown": np.ascontiguousarray(x[r * 2048:(r + 1) * 2048]),
        "wqk": wqk, "wv": wv, "bqk": bqk,
        "bv": np.ascontiguousarray(bv.T),
        "wo": wo_r, "bo": bo,
        "gq": np.asarray(full["L0_ffn_gate_qweight"]),
        "gsc": np.asarray(full["L0_ffn_gate_scales"]),
        "gz": np.asarray(full["L0_ffn_gate_zeros"]),
        "uq": np.asarray(full["L0_ffn_up_qweight"]),
        "usc": np.asarray(full["L0_ffn_up_scales"]),
        "uz": np.asarray(full["L0_ffn_up_zeros"]),
        "dq": np.asarray(full["L0_ffn_down_qweight"]),
        "dsc": np.asarray(full["L0_ffn_down_scales"]),
        "dz": np.asarray(full["L0_ffn_down_zeros"]),
        "eye": np.eye(128, dtype=np.float32),
        "maskbig": mask,
        "onescol": np.ones((128, 1), np.float32),
        "onesrow": np.ones((1, 128), np.float32),
        "shiftvec": (4 * (np.arange(128) % 8)).astype(np.int32).reshape(128, 1),
        "iota64": np.arange(64, dtype=np.float32).reshape(1, 64),
        "pos": np.arange(4096, dtype=np.float32).reshape(1, 4096),
        "oh86": oh86, "oh32": oh32, "rsel": rsel,
    }


def kernel(**inputs):
    nc = build(debug=0)
    in_maps = [make_inputs(c, inputs) for c in range(8)]
    res = run_bass_kernel_spmd(nc, in_maps, core_ids=list(range(8)))
    out = np.empty((4, S, E), np.float32)
    for c in range(8):
        b, r = c // 2, c % 2
        out[b, r * 2048:(r + 1) * 2048, :] = np.asarray(res.results[c]["out"])
    return out


# version 04 (final)
# speedup vs baseline: 1.1031x; 1.1031x over previous
import math
import numpy as np
from contextlib import ExitStack
import concourse.bass as bass
import concourse.mybir as mybir
from concourse.bass_utils import run_bass_kernel_spmd

F32 = mybir.dt.float32
F32R = mybir.dt.float32r
BF16 = mybir.dt.bfloat16
F16 = mybir.dt.float16
I32 = mybir.dt.int32
U8 = mybir.dt.uint8
AF = mybir.ActivationFunctionType
ALU = mybir.AluOpType

S, E = 4096, 4096
HL = 16
INTER = 11008
NG_H, NG_I = 32, 86
SC, NSC = 2048, 2
ROPE_BASE = 10000.0
ATT_SCALE = 1.0 / math.sqrt(128.0)
TWO_PI = 2.0 * math.pi
C1 = 6.28125
C2 = TWO_PI - C1
NCH = 43
ESZ = {BF16: 2, F16: 2, F32: 4, I32: 4, F32R: 4, U8: 1}


def build(debug=0, phases="ABCD"):
    nc = bass.Bass("TRN2", target_bir_lowering=False)

    x_in = nc.dram_tensor("x", [S, E], F32, kind="ExternalInput").ap()
    xown_in = nc.dram_tensor("xown", [2048, E], F32, kind="ExternalInput").ap()
    wqk_in = nc.dram_tensor("wqk", [32, E, 128], F32, kind="ExternalInput").ap()
    wv_in = nc.dram_tensor("wv", [E, 2048], F32, kind="ExternalInput").ap()
    bqk_in = nc.dram_tensor("bqk", [128, 32], F32, kind="ExternalInput").ap()
    bv_in = nc.dram_tensor("bv", [128, 16], F32, kind="ExternalInput").ap()
    wo_in = nc.dram_tensor("wo", [32, 32, 128, 128], F32, kind="ExternalInput").ap()
    bo_in = nc.dram_tensor("bo", [128, 32], F32, kind="ExternalInput").ap()
    gq_in = nc.dram_tensor("gq", [E // 8, INTER], I32, kind="ExternalInput").ap()
    gs_in = nc.dram_tensor("gsc", [NG_H, INTER], F16, kind="ExternalInput").ap()
    gz_in = nc.dram_tensor("gz", [NG_H, INTER // 8], I32, kind="ExternalInput").ap()
    uq_in = nc.dram_tensor("uq", [E // 8, INTER], I32, kind="ExternalInput").ap()
    us_in = nc.dram_tensor("usc", [NG_H, INTER], F16, kind="ExternalInput").ap()
    uz_in = nc.dram_tensor("uz", [NG_H, INTER // 8], I32, kind="ExternalInput").ap()
    dq_in = nc.dram_tensor("dq", [INTER // 8, E], I32, kind="ExternalInput").ap()
    ds_in = nc.dram_tensor("dsc", [NG_I, E], F16, kind="ExternalInput").ap()
    dz_in = nc.dram_tensor("dz", [NG_I, E // 8], I32, kind="ExternalInput").ap()
    eye_in = nc.dram_tensor("eye", [128, 128], F32, kind="ExternalInput").ap()
    mask_in = nc.dram_tensor("maskbig", [128, 1024], F32, kind="ExternalInput").ap()
    onesc_in = nc.dram_tensor("onescol", [128, 1], F32, kind="ExternalInput").ap()
    onesr_in = nc.dram_tensor("onesrow", [1, 128], F32, kind="ExternalInput").ap()
    shv_in = nc.dram_tensor("shiftvec", [128, 1], I32, kind="ExternalInput").ap()
    iota64_in = nc.dram_tensor("iota64", [1, 64], F32, kind="ExternalInput").ap()
    pos_in = nc.dram_tensor("pos", [1, S], F32, kind="ExternalInput").ap()
    oh86_in = nc.dram_tensor("oh86", [128, NG_I * NG_I], F32, kind="ExternalInput").ap()
    oh32_in = nc.dram_tensor("oh32", [128, NG_H * NG_H], F32, kind="ExternalInput").ap()
    rsel_in = nc.dram_tensor("rsel", [128, 2], F32, kind="ExternalInput").ap()

    out_t = nc.dram_tensor("out", [2048, E], F32, kind="ExternalOutput")

    kT_s = nc.dram_tensor("kT_s", [HL, 128, S], BF16)
    v_s = nc.dram_tensor("v_s", [HL, 128, 32, 128], BF16)
    q_s = nc.dram_tensor("q_s", [HL, 8, 128, 512], BF16)
    ag_in = [nc.dram_tensor(f"ag_in{p}", [HL * 4 * 128, 512], BF16) for p in range(2)]
    ag_out = [nc.dram_tensor(f"ag_out{p}", [2 * HL * 4 * 128, 512], BF16) for p in range(2)]
    wo_bf = nc.dram_tensor("wo_bf", [32, 32, 128, 128], BF16)
    xnew_s = nc.dram_tensor("xnew_s", [4, 128, 4 * E], F32)
    wg_sc = nc.dram_tensor("wg_sc", [128, NCH, NG_H, 256], BF16)
    wu_sc = nc.dram_tensor("wu_sc", [128, NCH, NG_H, 256], BF16)
    wd_sc = nc.dram_tensor("wd_sc", [128, 32, NG_I, 128], BF16)
    mz_g = nc.dram_tensor("mz_g", [NG_H, INTER], F32)
    mz_u = nc.dram_tensor("mz_u", [NG_H, INTER], F32)
    mz_d = nc.dram_tensor("mz_d", [NG_I, E], F32)
    s_gs = nc.dram_tensor("s_gs", [NG_H, INTER], F32)
    s_us = nc.dram_tensor("s_us", [NG_H, INTER], F32)
    s_ds = nc.dram_tensor("s_ds", [NG_I, E], F32)

    dbg = {}
    if debug:
        dbg["kT"] = nc.dram_tensor("dbg_kT", [HL, 128, S], BF16, kind="ExternalOutput")
        dbg["v"] = nc.dram_tensor("dbg_v", [HL, 128, 32, 128], BF16, kind="ExternalOutput")
        dbg["q"] = nc.dram_tensor("dbg_q", [HL, 8, 128, 512], BF16, kind="ExternalOutput")
        dbg["ag"] = nc.dram_tensor("dbg_ag", [2, 2 * HL * 4 * 128, 512], BF16,
                                   kind="ExternalOutput")
        dbg["xnew"] = nc.dram_tensor("dbg_xnew", [4, 128, 4 * E], F32, kind="ExternalOutput")
        dbg["wg"] = nc.dram_tensor("dbg_wg", [NG_H, 128, INTER], BF16, kind="ExternalOutput")
        dbg["mz"] = nc.dram_tensor("dbg_mz", [NG_H, INTER], F32, kind="ExternalOutput")
        dbg["xlnT_A"] = nc.dram_tensor("dbg_xlnT_A", [128, 65536], U8, kind="ExternalOutput")
        dbg["xlnT_B"] = nc.dram_tensor("dbg_xlnT_B", [128, 65536], U8, kind="ExternalOutput")

    es = ExitStack()
    T = lambda name, shape, dt_: es.enter_context(nc.sbuf_tensor(name, shape, dt_))

    bigA = T("bigA", [128, 65536], U8)
    bigB = T("bigB", [128, 65536], U8)
    arC = T("arC", [128, 49152], U8)
    arD = T("arD", [128, 16384], U8)

    def cv(t, off, n, dt_, p0=0, p1=128):
        return t[p0:p1, off:off + n * ESZ[dt_]].bitcast(dt_)

    idbf = T("idbf", [128, 128], BF16)
    idf16 = T("idf16", [128, 128], F16)
    idf32 = T("idf32", [128, 128], F32)
    onescol = T("onescol_t", [128, 1], F32)
    onesrow = T("onesrow_t", [1, 128], F32)
    shvec = T("shvec", [128, 1], I32)
    bqk_sb = T("bqk_sb", [128, 32], F32)
    bv_sb = T("bv_sb", [128, 16], F32)
    bo_sb = T("bo_sb", [128, 32], F32)
    epst = T("epst", [128, 1], F32)
    id32r = T("id32r", [32, 32], F32R)
    id86r = T("id86r", [NG_I, NG_I], F32R)
    rsel_sb = T("rsel_sb", [128, 2], F32)
    invf = T("invf", [1, 64], F32)
    iota64_sb = T("iota64_sb", [1, 64], F32)
    fifteen = T("fifteen", [128, 512], I32)
    xsum_sr = T("xsum_sr_t", [NG_H, 512], F32R)
    xsum2_sr = T("xsum2_sr_t", [NG_I, 512], F32R)
    ssr_one = T("ssr_one_t", [NG_I, 512], F32R)
    mzr0 = T("mzr0_t", [NG_I, 512], F32R)
    mzr1 = T("mzr1_t", [NG_H, 256], F32R)
    stats = T("stats", [128, 48], F32)
    mv = T("mv", [128, 2], F32)
    lnt1 = T("lnt1", [128, 1], F32)
    rstd = T("rstd", [128, 1], F32)

    pb01 = es.enter_context(nc.psum_tensor("pb01", [128, 1024], F32))
    pb = [pb01[:, 0:512], pb01[:, 512:1024]] + [
        es.enter_context(nc.psum_tensor(f"pb{i}", [128, 512], F32))[:] for i in range(2, 6)]
    ptbf = es.enter_context(nc.psum_tensor("ptbf", [128, 1024], BF16))
    ptf16 = es.enter_context(nc.psum_tensor("ptf16", [128, 1024], F16))

    class Sem:
        __slots__ = ("h", "n")

        def __init__(self, name):
            self.h = es.enter_context(nc.semaphore(name))
            self.n = 0

    SM = {n: Sem(n) for n in ["cT", "cV", "cS", "cCC"]}
    cT, cV, cS, cCC = SM["cT"], SM["cV"], SM["cS"], SM["cCC"]
    ALL_ENG = ["tensor", "vector", "scalar", "sync", "gpsimd"]
    prog = {k: [] for k in ALL_ENG}

    def em(engname, fn):
        prog[engname].append(fn)

    class Stm:
        __slots__ = ("sem", "rel")

        def __init__(self, name):
            self.sem = Sem(name)
            self.rel = None

    streams = {}

    def stm(name):
        if name not in streams:
            streams[name] = Stm(name)
        return streams[name]

    def wait(engname, sem, val=None):
        v = sem.n if val is None else val

        def f(e, sem=sem, v=v):
            e.wait_ge(sem.h, v)
        em(engname, f)

    def dma(q, name, out, in_, **kw):
        st = stm(name)
        if st.rel is not None:
            rs, rv = st.rel
            wait(q, rs, rv)
            st.rel = None
        st.sem.n += 16

        def f(e, out=out, in_=in_, kw=kw, sem=st.sem):
            e.dma_start(out=out, in_=in_, **kw).then_inc(sem.h, 16)
        em(q, f)
        return (st.sem, st.sem.n)

    def srel(name, sem):
        stm(name).rel = (sem, sem.n)

    def swait(engname, name):
        st = stm(name)
        if st.sem.n > 0:
            wait(engname, st.sem, st.sem.n)

    def barrier(engs=None):
        engs = engs or ALL_ENG
        for eng in engs:
            for s in SM.values():
                if s.n > 0:
                    wait(eng, s, s.n)
            for st in streams.values():
                if st.sem.n > 0:
                    wait(eng, st.sem, st.sem.n)

    def t_mm(out, lhsT, rhs, start, stop, sem=None):
        if sem is not None:
            sem.n += 1

        def f(t, out=out, lhsT=lhsT, rhs=rhs, start=start, stop=stop, sem=sem):
            i = t.matmul(out, lhsT, rhs, start=start, stop=stop)
            if sem is not None:
                i.then_inc(sem.h, 1)
        em("tensor", f)

    def t_tp(out, in_, iden, sem=None):
        if sem is not None:
            sem.n += 1

        def f(t, out=out, in_=in_, iden=iden, sem=sem):
            i = t.transpose(out, in_, iden)
            if sem is not None:
                i.then_inc(sem.h, 1)
        em("tensor", f)

    ENG_SEM = {"vector": cV, "scalar": cS}
    SELF_SER = True

    def mk_op(engname):
        esem = ENG_SEM.get(engname)

        def op(name, *args, sem=None, **kw):
            if esem is not None:
                if SELF_SER and esem.n > 0:
                    wait(engname, esem, esem.n)
                sem = esem
                sem.n += 1
            elif sem is not None:
                sem.n += 1

            def f(e, name=name, args=args, kw=kw, sem=sem):
                i = getattr(e, name)(*args, **kw)
                if sem is not None:
                    i.then_inc(sem.h, 1)
            em(engname, f)
        return op

    v_op, s_op, g_op = mk_op("vector"), mk_op("scalar"), mk_op("gpsimd")

    for dst, inp in [(idf32[:], eye_in), (onescol[:], onesc_in), (onesrow[:], onesr_in),
                     (shvec[:], shv_in), (bqk_sb[:], bqk_in), (bv_sb[:], bv_in),
                     (bo_sb[:], bo_in), (iota64_sb[:], iota64_in), (rsel_sb[:], rsel_in)]:
        dma("sync", "pro_c", dst, inp)
    dma("gpsimd", "pro_g", idbf[:], eye_in)
    dma("gpsimd", "pro_g", idf16[:], eye_in)
    for et in range(32):
        dma("gpsimd", "pro_w", wo_bf.ap()[et], wo_in[et], max_dma_last_dim=8192)
    dma("gpsimd", "pro_w", s_gs.ap(), gs_in, max_dma_last_dim=4096)
    dma("gpsimd", "pro_w", s_us.ap(), us_in, max_dma_last_dim=4096)
    dma("gpsimd", "pro_w", s_ds.ap(), ds_in, max_dma_last_dim=4096)
    swait("vector", "pro_c")
    swait("vector", "pro_g")
    v_op("memset", epst[:], 1e-5)
    v_op("memset", fifteen[:], 15)
    v_op("tensor_copy", id32r[:], idf32[0:32, 0:32])
    v_op("tensor_copy", id86r[:], idf32[0:NG_I, 0:NG_I], sem=cV)
    swait("scalar", "pro_c")
    s_op("activation", invf[:], iota64_sb[:], AF.Exp, scale=-math.log(ROPE_BASE) / 64.0, sem=cS)

    def emit_zprep(z_inp, mz_out, s_scr, ng, ncols):
        pk = ncols // 8
        zpak = cv(bigA, 0, pk, I32, 0, ng)
        zj = cv(bigA, pk * 4, pk, I32, 0, ng)
        ld0 = dma("sync", "pro_z", zpak, z_inp)
        nh = 2 if ncols * 8 > 65536 else 1
        hc = ncols // nh
        hpk = pk // nh
        for hf in range(nh):
            sall_h = cv(bigB, 0, hc, F32, 0, ng)
            mzsb_h = cv(bigB, hc * 4, hc, F32, 0, ng)
            if hf == 0:
                swait("sync", "pro_w")
            ld = dma("sync", "pro_z", sall_h, s_scr.ap()[:, hf * hc:(hf + 1) * hc])
            wait("vector", ld[0], ld[1])
            for j in range(8):
                v_op("tensor_scalar", zj[:, 0:hpk], zpak[:, hf * hpk:(hf + 1) * hpk],
                     4 * j, 15, ALU.logical_shift_right, ALU.bitwise_and)
                v_op("tensor_scalar",
                     mzsb_h.rearrange("g (m e) -> g m e", e=8)[:, :, j],
                     zj[:, 0:hpk], 1.0, -1.0, ALU.add, ALU.mult)
            v_op("tensor_tensor", mzsb_h, mzsb_h, sall_h, ALU.mult, sem=cV)
            wait("gpsimd", cV)
            dma("gpsimd", "pro_mz", mz_out.ap()[:, hf * hc:(hf + 1) * hc], mzsb_h)
            swait("vector", "pro_mz")
            srel("pro_z", cV)

    emit_zprep(gz_in, mz_g, s_gs, NG_H, INTER)
    emit_zprep(uz_in, mz_u, s_us, NG_H, INTER)
    emit_zprep(dz_in, mz_d, s_ds, NG_I, E)
    if debug:
        swait("sync", "pro_mz")
        dma("sync", "pro_c", dbg["mz"].ap(), mz_g.ap())
    barrier()

    def xlnT_tile(ech, col, n):
        t = bigA if ech < 16 else bigB
        return cv(t, (ech % 16) * SC * 2 + col * 2, n, BF16)

    def xlnT_8(e8, tt):
        t = bigA if e8 < 2 else bigB
        base = (e8 % 2) * 8 * SC * 2
        return cv(t, base, 8 * SC, BF16).rearrange(
            "p (ec t) -> p ec t", t=SC)[:, :, tt * 128:(tt + 1) * 128]

    x_f32 = [cv(arC, i * 16384, E, F32) for i in range(2)]
    xln_bf = cv(arC, 32768, E, BF16)
    wv_buf = [cv(arC, i * 16384, 32 * 256, BF16) for i in range(2)]
    v_out = [cv(arC, 32768 + i * 512, 256, BF16) for i in range(2)]
    wqk_buf = [cv(arC, i * 8192, 4096, BF16) for i in range(2)]
    rope_t1 = [cv(arC, 16384 + i * 2048, 512, F32) for i in range(2)]
    rope_t2 = cv(arC, 20480, 512, F32)
    rope_m1 = cv(arC, 22528, 512, F32)
    rope_m2 = cv(arC, 24576, 512, F32)
    qk_out = [cv(arC, 26624 + i * 1024, 512, BF16) for i in range(2)]
    cs_c = cv(arC, 40960, SC, BF16)
    cs_s = cv(arC, 45056, SC, BF16)
    tbl_t = [cv(arD, i * 2048, 512, F32, 0, 64) for i in range(4)]
    tbl_i = cv(arD, 8192, 512, I32, 0, 64)
    posb = cv(arD, 10240, 512, F32, 0, 1)

    def emit_tables(sc, brel):
        for sl in range(4):
            ld = dma("sync", "a_pos", posb,
                     pos_in[:, sc * SC + sl * 512: sc * SC + (sl + 1) * 512])
            wait("tensor", ld[0], ld[1])
            rs, rv = brel[4]
            wait("tensor", rs, rv)
            t_mm(pb[4][0:64, :], invf[:], posb, True, True, sem=cT)
            srel("a_pos", cT)
            wait("vector", cT)
            ang = pb[4][0:64, :]
            v_op("tensor_scalar", tbl_i, ang, 1.0 / TWO_PI, None, ALU.mult, ALU.bypass)
            v_op("tensor_copy", tbl_t[0], tbl_i)
            v_op("tensor_scalar", tbl_t[1], tbl_t[0], C1, None, ALU.mult, ALU.bypass)
            v_op("tensor_tensor", tbl_t[1], ang, tbl_t[1], ALU.subtract)
            v_op("tensor_scalar", tbl_t[2], tbl_t[0], C2, None, ALU.mult, ALU.bypass)
            v_op("tensor_tensor", tbl_t[1], tbl_t[1], tbl_t[2], ALU.subtract, sem=cV)
            brel[4] = (cV, cV.n)
            wait("scalar", cV)
            s_op("activation", tbl_t[2], tbl_t[1], AF.Sin, scale=0.5)
            s_op("activation", tbl_t[3], tbl_t[1], AF.Sin, scale=0.25, sem=cS)
            wait("vector", cS)
            sl_c = cs_c[0:64, sl * 512:(sl + 1) * 512]
            sl_s = cs_s[64:128, sl * 512:(sl + 1) * 512]
            v_op("tensor_tensor", tbl_t[3], tbl_t[3], tbl_t[3], ALU.mult)
            v_op("tensor_scalar", tbl_t[3], tbl_t[3], -2.0, 1.0, ALU.mult, ALU.add)
            v_op("tensor_tensor", tbl_t[0], tbl_t[2], tbl_t[2], ALU.mult)
            v_op("tensor_scalar", sl_c, tbl_t[0], -2.0, 1.0, ALU.mult, ALU.add)
            v_op("tensor_tensor", tbl_t[2], tbl_t[2], tbl_t[3], ALU.mult)
            v_op("tensor_scalar", sl_s, tbl_t[2], 2.0, None, ALU.mult, ALU.bypass)
            v_op("tensor_copy", cs_c[64:128, sl * 512:(sl + 1) * 512], sl_c)
            v_op("tensor_scalar", cs_s[0:64, sl * 512:(sl + 1) * 512], sl_s, -1.0, None,
                 ALU.mult, ALU.bypass, sem=cV)
        return cV.n

    def emit_phaseA():
        brel = {i: (cS, 0) for i in range(6)}
        for sc in range(NSC):
            for tt in range(16):
                ttg = sc * 16 + tt
                xb = x_f32[tt % 2]
                ld = dma("sync", f"a_x{tt % 2}", xb, x_in[ttg * 128:(ttg + 1) * 128, :])
                wait("vector", ld[0], ld[1])
                for i in range(8):
                    v_op("bn_stats", stats[:, i * 6:(i + 1) * 6], xb[:, i * 512:(i + 1) * 512])
                v_op("bn_aggr", mv[:], stats[:], sem=cV)
                wait("scalar", cV)
                s_op("activation", lnt1[:], mv[:, 1:2], AF.Ln, bias=epst[:])
                s_op("activation", rstd[:], lnt1[:], AF.Exp, scale=-0.5, sem=cS)
                wait("vector", cS)
                v_op("tensor_scalar", xln_bf, xb, mv[:, 0:1], rstd[:],
                     ALU.subtract, ALU.mult, sem=cV)
                srel(f"a_x{tt % 2}", cV)
                for e8 in range(4):
                    wait("tensor", cV)
                    for j in range(8):
                        c = e8 * 8 + j
                        t_tp(ptbf[:, j * 128:(j + 1) * 128], xln_bf[:, c * 128:(c + 1) * 128],
                             idbf[:], sem=(cT if j == 7 else None))
                    wait("vector", cT)
                    v_op("tensor_copy", xlnT_8(e8, tt),
                         ptbf[:].rearrange("p (a b) -> p a b", b=128), sem=cV)
            xlnT_V = cV.n
            if debug and sc == 0:
                wait("sync", cV, xlnT_V)
                dma("sync", "pro_c", dbg["xlnT_A"].ap(), bigA[:, :])
                dma("sync", "pro_c", dbg["xlnT_B"].ap(), bigB[:, :])
                swait("gpsimd", "pro_c")
                swait("vector", "pro_c")
                swait("tensor", "pro_c")
            tbl_V = emit_tables(sc, brel)
            for vb in range(8):
                wb = wv_buf[vb % 2]
                if vb < 2:
                    stm(f"a_wv{vb % 2}").rel = (cV, xlnT_V)
                ld = dma("gpsimd", f"a_wv{vb % 2}", wb,
                         wv_in.rearrange("(ec p) f -> p ec f", p=128)
                         [:, :, vb * 256:(vb + 1) * 256])
                wait("tensor", ld[0], ld[1])
                wait("tensor", cV, xlnT_V)
                for tt in range(16):
                    ttg = sc * 16 + tt
                    bank = 4 + (tt % 2)
                    rsb, rvb = brel[bank]
                    wait("tensor", rsb, rvb)
                    ps = pb[bank][:, 0:256]
                    for ech in range(32):
                        t_mm(ps, xlnT_tile(ech, tt * 128, 128), wb[:, ech * 256:(ech + 1) * 256],
                             ech == 0, ech == 31, sem=(cT if ech == 31 else None))
                    wait("scalar", cT)
                    vo = v_out[tt % 2]
                    swait("scalar", f"a_vo{tt % 2}")
                    s_op("copy", vo, ps, sem=cS)
                    brel[bank] = (cS, cS.n)
                    wait("gpsimd", cS)
                    dma("gpsimd", f"a_vo{tt % 2}",
                        v_s.ap()[vb * 2:vb * 2 + 2, :, ttg, :].rearrange("h p d -> p h d"),
                        vo.rearrange("p (h d) -> p h d", d=128))
                srel(f"a_wv{vb % 2}", cT)
            for ft in range(32):
                wb = wqk_buf[ft % 2]
                if ft < 2:
                    stm(f"a_wqk{ft % 2}").rel = (cT, cT.n)
                ld = dma("gpsimd", f"a_wqk{ft % 2}", wb,
                         wqk_in[ft].rearrange("(ec p) f -> p ec f", p=128))
                wait("tensor", ld[0], ld[1])
                for tc in range(4):
                    bank = tc
                    rsb, rvb = brel[bank]
                    wait("tensor", rsb, rvb)
                    ps = pb[bank][:]
                    for ech in range(32):
                        t_mm(ps, wb[:, ech * 128:(ech + 1) * 128], xlnT_tile(ech, tc * 512, 512),
                             ech == 0, ech == 31, sem=(cT if ech == 31 else None))
                    wait("scalar", cT)
                    t1 = rope_t1[tc % 2]
                    s_op("activation", t1, ps, AF.Identity, bias=bqk_sb[:, ft:ft + 1],
                         scale=1.0, sem=cS)
                    brel[bank] = (cS, cS.n)
                    wait("vector", cS)
                    if ft == 0 and tc == 0:
                        wait("vector", cV, tbl_V)
                    v_op("tensor_copy", rope_t2[0:64, :], t1[64:128, :])
                    v_op("tensor_copy", rope_t2[64:128, :], t1[0:64, :])
                    csl = slice(tc * 512, (tc + 1) * 512)
                    v_op("tensor_tensor", rope_m1, t1, cs_c[:, csl], ALU.mult, sem=cV)
                    v_op("tensor_tensor", rope_m2, rope_t2, cs_s[:, csl], ALU.mult)
                    qo = qk_out[tc % 2]
                    swait("vector", f"a_qo{tc % 2}")
                    v_op("tensor_tensor", qo, rope_m1, rope_m2, ALU.add, sem=cV)
                    wait("gpsimd", cV)
                    if ft < 16:
                        dma("gpsimd", f"a_qo{tc % 2}", q_s.ap()[ft, sc * 4 + tc], qo)
                    else:
                        dma("gpsimd", f"a_qo{tc % 2}",
                            kT_s.ap()[ft - 16][:, sc * SC + tc * 512: sc * SC + (tc + 1) * 512],
                            qo)
                srel(f"a_wqk{ft % 2}", cT)
            barrier()
        if debug:
            for h in range(HL):
                dma("sync", "pro_c", dbg["kT"].ap()[h], kT_s.ap()[h])
                dma("sync", "pro_c", dbg["q"].ap()[h], q_s.ap()[h])
                dma("sync", "pro_c", dbg["v"].ap()[h], v_s.ap()[h])

    def emit_phaseB():
        kv_k = [cv(arC, i * 8192, S, BF16) for i in range(2)]
        kv_v = [cv(arC, 16384 + i * 8192, S, BF16) for i in range(2)]
        q_buf = [cv(arC, 32768 + i * 1024, 512, BF16) for i in range(2)]
        probs = cv(arC, 34816, 1024, BF16)
        ctx_sb = [cv(arC, 36864 + i * 1024, 512, BF16) for i in range(2)]
        acc_sb = cv(arC, 38912, 512, F32)
        rb_sb = cv(arC, 40960, 512, F32)
        rec_sb = cv(arC, 43008, 512, F32, 0, 1)
        maskbf = cv(arC, 45056, 1024, BF16)
        barrier()
        mld = dma("gpsimd", "b_msk", maskbf, mask_in)
        wait("vector", mld[0], mld[1])
        accb = [pb[4], ptf16[:].bitcast(F32)[:, 0:512]]
        pv_rel = {0: (cV, 0), 1: (cV, 0)}
        acc_rel = {0: (cV, 0), 1: (cV, 0)}
        sc_rel = {0: (cS, 0), 1: (cS, 0)}
        prb_rel = {0: (cV, 0), 1: (cV, 0)}
        pend = None
        qcg = 0

        def finalize(p):
            h_, par, pv_T, _pss, _qq = p
            wait("vector", cT, pv_T)
            v_op("tensor_copy", acc_sb, accb[par], sem=cV)
            acc_rel[par] = (cV, cV.n)
            wait("tensor", cV)
            t_mm(pb[5][0:1, :], onescol[:], acc_sb, True, True, sem=cT)
            wait("vector", cT)
            v_op("reciprocal", rec_sb, pb[5][0:1, :], sem=cV)
            wait("tensor", cV)
            t_mm(pb[5], onesrow[:], rec_sb, True, True, sem=cT)
            wait("vector", cT)
            v_op("tensor_copy", rb_sb, pb[5], sem=cV)
            cx = ctx_sb[par]
            swait("vector", f"b_cx{par}")
            v_op("tensor_tensor", rb_sb, pb[2 + par], rb_sb, ALU.mult)
            v_op("tensor_scalar", cx, rb_sb, bv_sb[:, h_:h_ + 1], None,
                 ALU.add, ALU.bypass, sem=cV)
            pv_rel[par] = (cV, cV.n)
            wait("gpsimd", cV)
            dma("gpsimd", f"b_cx{par}",
                ag_in[p[3]].ap().rearrange("(h q p) t -> h q p t", q=4, p=128)[h_, p[4]],
                cx)

        for pss in range(2):
            for h in range(HL):
                ib = h % 2
                dma("sync", f"b_kv{ib}", kv_k[ib], kT_s.ap()[h])
                kvt = dma("sync", f"b_kv{ib}", kv_v[ib],
                          v_s.ap()[h].rearrange("p tt d -> p (tt d)"))
                for qq in range(4):
                    qc = pss * 4 + qq
                    par = qcg % 2
                    qb = q_buf[qc % 2]
                    qt = dma("sync", f"b_q{qc % 2}", qb, q_s.ap()[h, qc])
                    nkt = 4 * qc + 4
                    wait("tensor", qt[0], qt[1])
                    wait("tensor", kvt[0], kvt[1])
                    kt = 0
                    first_pv = True
                    while kt < nkt:
                        npair = min(2, nkt - kt)
                        for u in range(npair):
                            k_ = kt + u
                            rs2, rv2 = sc_rel[u]
                            wait("tensor", rs2, rv2)
                            t_mm(pb[u], kv_k[ib][:, k_ * 128:(k_ + 1) * 128], qb, True, True,
                                 sem=cT)
                        wait("scalar", cT)
                        for u in range(npair):
                            rs3, rv3 = prb_rel[u]
                            wait("scalar", rs3, rv3)
                        if npair == 2:
                            s_op("activation", probs[:, 0:1024], pb01[:], AF.Exp,
                                 scale=ATT_SCALE, sem=cS)
                        else:
                            s_op("activation", probs[:, 0:512], pb[0], AF.Exp,
                                 scale=ATT_SCALE, sem=cS)
                        sc_rel[0] = (cS, cS.n)
                        sc_rel[1] = (cS, cS.n)
                        wait("vector", cS)
                        diag_T = {}
                        for u in range(npair):
                            k_ = kt + u
                            pr = probs[:, u * 512:(u + 1) * 512]
                            r = k_ - 4 * qc
                            if r >= 0:
                                v_op("tensor_tensor", pr, pr,
                                     maskbf[:, 512 - 128 * r: 1024 - 128 * r], ALU.mult)
                                diag_T[u] = cV.n
                            if k_ == 0:
                                rsa, rva = acc_rel[par]
                                wait("vector", rsa, rva)
                                v_op("tensor_copy", accb[par], pr, sem=cV)
                            else:
                                v_op("tensor_tensor", accb[par], pr, accb[par], ALU.add,
                                     sem=cV)
                            prb_rel[u] = (cV, cV.n)
                        for u in range(npair):
                            k_ = kt + u
                            pr = probs[:, u * 512:(u + 1) * 512]
                            if first_pv:
                                rs4, rv4 = pv_rel[par]
                                wait("tensor", rs4, rv4)
                                first_pv = False
                            if u in diag_T:
                                wait("tensor", cV, diag_T[u])
                            else:
                                wait("tensor", cS)
                            t_mm(pb[2 + par], kv_v[ib][:, k_ * 128:(k_ + 1) * 128], pr,
                                 k_ == 0, k_ == nkt - 1, sem=(cT if k_ == nkt - 1 else None))
                        kt += npair
                    pv_T = cT.n
                    srel(f"b_q{qc % 2}", cT)
                    if pend is not None:
                        finalize(pend)
                    pend = (h, par, pv_T, pss, qq)
                    qcg += 1
                srel(f"b_kv{ib}", cT)
            finalize(pend)
            pend = None
            swait("gpsimd", "b_cx0")
            swait("gpsimd", "b_cx1")
            cCC.n += 1

            def fcc(g, pss_=pss):
                g.collective_compute(
                    "AllGather", ALU.bypass,
                    replica_groups=[[0, 1], [2, 3], [4, 5], [6, 7]],
                    ins=[ag_in[pss_].ap().opt()], outs=[ag_out[pss_].ap().opt()],
                ).then_inc(cCC.h, 1)
            em("gpsimd", fcc)
        if debug:
            wait("sync", cCC, 2)
            for p in range(2):
                dma("sync", "pro_c", dbg["ag"].ap()[p], ag_out[p].ap())

    def emit_phaseCD():
        ctx_all = [cv(bigA, i * 32768, 32 * 512, BF16) for i in range(2)]
        xnew = cv(bigB, 0, 4 * E, F32)
        x2T = cv(bigA, 0, 32 * 512, BF16)
        x2bf = cv(bigA, 32768, E, BF16)
        oh86_sb = cv(bigA, 0, NG_I * NG_I, BF16)
        oh32_sb = cv(bigA, 63488, NG_H * NG_H, BF16)
        stg_f = cv(bigA, 36864, 512, F32, 0, NG_I)
        wo_buf = [cv(arC, i * 8192, 32 * 128, BF16) for i in range(2)]
        to_sb = [cv(arC, 16384 + i * 2048, 512, F32) for i in range(2)]
        xn_slC = [cv(arC, 20480 + i * 2048, 512, F32) for i in range(2)]
        wgu = [cv(arC, 0, NG_H * 256, BF16), cv(arC, 16384, NG_H * 256, BF16),
               cv(arC, 32768, NG_H * 256, BF16)]
        wd_buf = [cv(arC, i * 22016, NG_I * 128, BF16) for i in range(2)]
        xn_slD = [cv(arC, 44032 + i * 2048, 512, F32) for i in range(2)]
        pk_buf = [cv(arD, i * 2048, 512, I32) for i in range(2)]
        nib_buf = cv(arD, 4096, 512, I32)
        wd_dq = cv(arD, 6144, 512, BF16)
        silu_sb = [cv(arD, 8192 + i * 2048, 512, F32) for i in range(2)]
        o_sb = [cv(arD, 8192 + i * 2048, 512, F32) for i in range(2)]
        d16_sb = [cv(arD, 12288 + i * 1024, 512, F16) for i in range(2)]

        def interT_tile(it):
            if it < 64:
                return cv(bigB, it * 1024, 512, BF16)
            return cv(bigA, 40960 + (it - 64) * 1024, 512, BF16)

        barrier()
        wait("sync", cCC, 2)
        wait("gpsimd", cCC, 2)
        for tci in range(4):
            for blk in range(2):
                dma("sync", f"c_ctx{blk}", ctx_all[blk],
                    ag_out[blk].ap().rearrange("(r h q p) t -> r h q p t", h=HL, q=4, p=128)
                    [:, :, tci].rearrange("r h p t -> p r h t"))
            swait("vector", "c_ctx0")
            swait("vector", "c_ctx1")
            v_op("tensor_scalar", ctx_all[1], ctx_all[1], rsel_sb[:, 1:2], None,
                 ALU.mult, ALU.bypass)
            v_op("tensor_scalar", ctx_all[0], ctx_all[0], rsel_sb[:, 0:1], None,
                 ALU.mult, ALU.bypass)
            v_op("tensor_tensor", ctx_all[0], ctx_all[0], ctx_all[1], ALU.add, sem=cV)
            ctx_V = cV.n
            srel("c_ctx1", cV)
            stm("d_oh32").rel = (cV, ctx_V)
            o32 = dma("gpsimd", "d_oh32", oh32_sb, oh32_in)
            o_psum_rel = {0: (cS, 0), 1: (cS, 0)}
            tp_rel = (cV, 0)
            for et in range(32):
                wb = wo_buf[et % 2]
                wl = dma("gpsimd", f"c_wo{et % 2}", wb,
                         wo_bf.ap()[et].rearrange("ct p f -> p ct f"))
                xl = dma("sync", f"c_xo{et % 2}", xn_slC[et % 2],
                         xown_in.rearrange("(tc j p) e -> tc j p e", j=4, p=128)
                         [tci, :, :, et * 128:(et + 1) * 128].rearrange("j p e -> p j e"))
                wait("tensor", wl[0], wl[1])
                wait("tensor", cV, ctx_V)
                bank = et % 2
                rs2, rv2 = o_psum_rel[bank]
                wait("tensor", rs2, rv2)
                for ct in range(32):
                    t_mm(pb[bank][:], wb[:, ct * 128:(ct + 1) * 128],
                         ctx_all[0][:, ct * 512:(ct + 1) * 512],
                         ct == 0, ct == 31, sem=(cT if ct == 31 else None))
                srel(f"c_wo{et % 2}", cT)
                wait("scalar", cT)
                swait_to = to_sb[et % 2]
                s_op("activation", swait_to, pb[bank][:], AF.Identity,
                     bias=bo_sb[:, et:et + 1], scale=1.0, sem=cS)
                o_psum_rel[bank] = (cS, cS.n)
                wait("tensor", cS)
                rs4, rv4 = tp_rel
                wait("tensor", rs4, rv4)
                for j in range(4):
                    t_tp(pb[2][:, j * 128:(j + 1) * 128],
                         to_sb[et % 2][:, j * 128:(j + 1) * 128],
                         idf32[:], sem=(cT if j == 3 else None))
                wait("vector", cT)
                wait("vector", xl[0], xl[1])
                v_op("tensor_tensor",
                     xnew.rearrange("p (j e) -> p j e", e=E)[:, :, et * 128:(et + 1) * 128],
                     pb[2][:].rearrange("p (j e) -> p j e", e=128),
                     xn_slC[et % 2].rearrange("p (j e) -> p j e", e=128),
                     ALU.add, sem=cV)
                tp_rel = (cV, cV.n)
                srel(f"c_xo{et % 2}", cV)
            ctx_done_T = cT.n
            srel("c_ctx0", cT)
            for j in range(4):
                xnj = xnew.rearrange("p (j e) -> p j e", e=E)[:, j, :]
                wait("vector", cV)
                for i in range(8):
                    v_op("bn_stats", stats[:, i * 6:(i + 1) * 6], xnj[:, i * 512:(i + 1) * 512])
                v_op("bn_aggr", mv[:], stats[:], sem=cV)
                wait("scalar", cV)
                s_op("activation", lnt1[:], mv[:, 1:2], AF.Ln, bias=epst[:])
                s_op("activation", rstd[:], lnt1[:], AF.Exp, scale=-0.5, sem=cS)
                wait("vector", cS)
                wait("vector", cT, ctx_done_T)
                v_op("tensor_scalar", x2bf, xnj, mv[:, 0:1], rstd[:],
                     ALU.subtract, ALU.mult, sem=cV)
                for e8 in range(4):
                    wait("tensor", cV)
                    for jj in range(8):
                        c = e8 * 8 + jj
                        t_tp(ptbf[:, jj * 128:(jj + 1) * 128], x2bf[:, c * 128:(c + 1) * 128],
                             idbf[:], sem=(cT if jj == 7 else None))
                    wait("vector", cT)
                    v_op("tensor_copy",
                         x2T.rearrange("p (ec t) -> p ec t", t=512)
                         [:, e8 * 8:e8 * 8 + 8, j * 128:(j + 1) * 128],
                         ptbf[:].rearrange("p (a b) -> p a b", b=128), sem=cV)
            x2T_V = cV.n
            wait("gpsimd", cV, x2T_V)
            spl = dma("gpsimd", "x_spill", xnew_s.ap()[tci], xnew)
            if debug:
                dma("gpsimd", "x_spill", dbg["xnew"].ap()[tci], xnew)
            wait("tensor", cV, x2T_V)
            wait("tensor", o32[0], o32[1])
            for g in range(NG_H):
                t_mm(pb[5][0:NG_H, :], oh32_sb[:, g * NG_H:(g + 1) * NG_H],
                     x2T[:, g * 512:(g + 1) * 512], g == 0, g == NG_H - 1,
                     sem=(cT if g == NG_H - 1 else None))
            srel("d_oh32", cT)
            wait("vector", cT)
            v_op("tensor_copy", xsum_sr[:], pb[5][0:NG_H, :], sem=cV)
            xsum_V = cV.n
            swait("tensor", "x_spill")
            swait("vector", "x_spill")
            swait("sync", "x_spill")
            swait("gpsimd", "x_spill")
            gu_rel = {0: (cS, 0), 1: (cS, 0), 2: (cV, 0), 3: (cV, 0)}
            sb_rel = {4: (cV, 0), 5: (cV, 0)}
            itg = 0
            for ci in range(NCH):
                coff = ci * 256
                bufs = []
                for mi, (mat_pk, mat_s, mat_mz, mat_sc) in enumerate(
                        [(gq_in, s_gs, mz_g, wg_sc), (uq_in, s_us, mz_u, wu_sc)]):
                    bname = f"d_wg{ci % 2}" if mi == 0 else "d_wu"
                    wb = wgu[(ci % 2) if mi == 0 else 2]
                    if tci == 0:
                        sld = dma("sync", "d_s", stg_f[0:NG_H, 0:256],
                                  mat_s.ap()[:, coff:coff + 256])
                        wait("vector", sld[0], sld[1])
                        v_op("tensor_copy", ssr_one[0:NG_H, mi * 256:(mi + 1) * 256],
                             stg_f[0:NG_H, 0:256], sem=cV)
                        srel("d_s", cV)
                        srdy = cV.n
                        st_ = stm(bname)
                        if st_.rel is not None:
                            wait("vector", st_.rel[0], st_.rel[1])
                            st_.rel = None
                        for g in range(NG_H):
                            pld = dma("sync", f"d_pk{g % 2}", pk_buf[g % 2][:, 0:256],
                                      mat_pk[g * 16:(g + 1) * 16, coff:coff + 256]
                                      .rearrange("r (o c) -> r o c", o=1)
                                      .to_broadcast([16, 8, 256]))
                            wait("vector", pld[0], pld[1])
                            v_op("scalar_tensor_tensor", nib_buf[:, 0:256],
                                 pk_buf[g % 2][:, 0:256], shvec[:], fifteen[:, 0:256],
                                 ALU.logical_shift_right, ALU.bitwise_and)
                            srel(f"d_pk{g % 2}", cV)
                            bank = 4 + (g % 2)
                            rsb, rvb = sb_rel[bank]
                            wait("tensor", rsb, rvb)
                            wait("tensor", cV, srdy)
                            t_mm(pb[bank][:, 0:256],
                                 id32r[:, g:g + 1].to_broadcast([NG_H, 128]),
                                 ssr_one[0:NG_H, mi * 256:(mi + 1) * 256], True, True, sem=cT)
                            wait("vector", cT)
                            v_op("tensor_tensor", wb[:, g * 256:(g + 1) * 256],
                                 nib_buf[:, 0:256], pb[bank][:, 0:256], ALU.mult, sem=cV)
                            sb_rel[bank] = (cV, cV.n)
                        wdone = cV.n
                        wait("gpsimd", cV, wdone)
                        dma("gpsimd", "d_wst", mat_sc.ap()[:, ci], wb)
                        bufs.append((wb, (cV, wdone), bname))
                    else:
                        wld = dma("sync", bname, wb, mat_sc.ap()[:, ci])
                        bufs.append((wb, wld, bname))
                for mi, mat_mz in enumerate([mz_g, mz_u]):
                    mld = dma("sync", "d_s", stg_f[0:NG_H, 0:256],
                              mat_mz.ap()[:, coff:coff + 256])
                    wait("vector", mld[0], mld[1])
                    mzdst = mzr0[0:NG_H, 0:256] if mi == 0 else mzr1[0:NG_H, 0:256]
                    swait("vector", f"d_mzg{mi}")
                    v_op("tensor_copy", mzdst, stg_f[0:NG_H, 0:256], sem=cV)
                    srel("d_s", cV)
                mz_V = cV.n
                for it in range(2):
                    for mi in range(2):
                        wb, wrel, bname = bufs[mi]
                        bank = (0 + itg % 2) if mi == 0 else (2 + itg % 2)
                        rs2, rv2 = gu_rel[bank]
                        wait("tensor", rs2, rv2)
                        wait("tensor", wrel[0], wrel[1])
                        wait("tensor", cV, mz_V)
                        wait("tensor", cV, xsum_V)
                        for g in range(NG_H):
                            t_mm(pb[bank][:], wb[:, g * 256 + it * 128: g * 256 + it * 128 + 128],
                                 x2T[:, g * 512:(g + 1) * 512], g == 0, False)
                        mzsrc = mzr0 if mi == 0 else mzr1
                        t_mm(pb[bank][:], mzsrc[0:NG_H, it * 128:it * 128 + 128],
                             xsum_sr[0:NG_H, :], False, True, sem=cT)
                    wait("scalar", cT)
                    sl = silu_sb[itg % 2]
                    s_op("activation", sl, pb[0 + itg % 2][:], AF.Silu, sem=cS)
                    gu_rel[0 + itg % 2] = (cS, cS.n)
                    wait("vector", cS)
                    v_op("tensor_tensor", interT_tile(itg), sl, pb[2 + itg % 2][:], ALU.mult,
                         sem=cV)
                    gu_rel[2 + itg % 2] = (cV, cV.n)
                    itg += 1
                for _, _, bname in bufs:
                    srel(bname, cT)
            inter_V = cV.n
            wait("gpsimd", cT)
            ol = dma("gpsimd", "d_oh86", oh86_sb, oh86_in)
            wait("tensor", ol[0], ol[1])
            wait("tensor", cV, inter_V)
            for g in range(NG_I):
                t_mm(pb[5][0:NG_I, :], oh86_sb[:, g * NG_I:(g + 1) * NG_I],
                     interT_tile(g), g == 0, g == NG_I - 1, sem=(cT if g == NG_I - 1 else None))
            srel("d_oh86", cT)
            wait("vector", cT)
            v_op("tensor_copy", xsum2_sr[:], pb[5][0:NG_I, :], sem=cV)
            xsum2_V = cV.n
            barrier(["sync", "gpsimd", "vector"])
            d_rel = {0: (cV, 0), 1: (cV, 0)}
            mzd_V = 0
            for ec in range(8):
                if tci == 0:
                    sld = dma("sync", "d_s", stg_f[0:NG_I, 0:512],
                              s_ds.ap()[:, ec * 512:(ec + 1) * 512])
                    wait("vector", sld[0], sld[1])
                    v_op("tensor_copy", ssr_one[0:NG_I, 0:512],
                         stg_f[0:NG_I, 0:512], sem=cV)
                    srel("d_s", cV)
                    srdy2 = cV.n
                    for g in range(NG_I):
                        pld = dma("sync", f"d_pk{g % 2}", pk_buf[g % 2],
                                  dq_in[g * 16:(g + 1) * 16, ec * 512:(ec + 1) * 512]
                                  .rearrange("r (o c) -> r o c", o=1)
                                  .to_broadcast([16, 8, 512]))
                        wait("vector", pld[0], pld[1])
                        v_op("scalar_tensor_tensor", nib_buf[:, 0:512], pk_buf[g % 2],
                             shvec[:], fifteen[:], ALU.logical_shift_right, ALU.bitwise_and)
                        srel(f"d_pk{g % 2}", cV)
                        bank = 4 + (g % 2)
                        rsb, rvb = sb_rel[bank]
                        wait("tensor", rsb, rvb)
                        wait("tensor", cV, srdy2)
                        t_mm(pb[bank][:], id86r[:, g:g + 1].to_broadcast([NG_I, 128]),
                             ssr_one[0:NG_I, 0:512], True, True, sem=cT)
                        wait("vector", cT)
                        swait("vector", "d_wst")
                        v_op("tensor_tensor", wd_dq, nib_buf[:, 0:512], pb[bank][:],
                             ALU.mult, sem=cV)
                        sb_rel[bank] = (cV, cV.n)
                        wait("gpsimd", cV)
                        dma("gpsimd", "d_wst",
                            wd_sc.ap()[:, 4 * ec:4 * ec + 4, g, :], wd_dq)
                    swait("sync", "d_wst")
                for et4 in range(4):
                    et = ec * 4 + et4
                    bi = et % 2
                    wld = dma("sync", f"d_wd{bi}", wd_buf[bi], wd_sc.ap()[:, et])
                    wait("sync", cV, mzd_V)
                    mld = dma("sync", "d_s", stg_f[0:NG_I, 0:128],
                              mz_d.ap()[:, et * 128:(et + 1) * 128])
                    wait("vector", mld[0], mld[1])
                    v_op("tensor_copy", mzr0[0:NG_I, 0:128],
                         stg_f[0:NG_I, 0:128], sem=cV)
                    srel("d_s", cV)
                    mzd_V = cV.n
                    xld = dma("sync", f"d_xn{et % 2}", xn_slD[et % 2],
                              xnew_s.ap()[tci].rearrange("p (j e) -> p j e", e=E)
                              [:, :, et * 128:(et + 1) * 128])
                    bank = et % 2
                    rs2, rv2 = d_rel[bank]
                    wait("tensor", rs2, rv2)
                    wait("tensor", wld[0], wld[1])
                    wait("tensor", cV, mzd_V)
                    wait("tensor", cV, xsum2_V)
                    for g in range(NG_I):
                        t_mm(pb[bank][:], wd_buf[bi][:, g * 128:(g + 1) * 128],
                             interT_tile(g), g == 0, False)
                    t_mm(pb[bank][:], mzr0[0:NG_I, 0:128], xsum2_sr[0:NG_I, :],
                         False, True, sem=cT)
                    srel(f"d_wd{bi}", cT)
                    wait("vector", cT)
                    d16 = d16_sb[et % 2]
                    v_op("tensor_copy", d16, pb[bank][:], sem=cV)
                    d_rel[bank] = (cV, cV.n)
                    wait("tensor", cV)
                    for j in range(4):
                        t_tp(ptf16[:, j * 128:(j + 1) * 128], d16[:, j * 128:(j + 1) * 128],
                             idf16[:], sem=(cT if j == 3 else None))
                    wait("vector", cT)
                    wait("vector", xld[0], xld[1])
                    ob = o_sb[et % 2]
                    swait("vector", f"d_out{et % 2}")
                    v_op("tensor_tensor", ob, ptf16[:, 0:512], xn_slD[et % 2], ALU.add, sem=cV)
                    srel(f"d_xn{et % 2}", cV)
                    wait("gpsimd", cV)
                    dma("gpsimd", f"d_out{et % 2}",
                        out_t.ap().rearrange("(tc j p) e -> tc j p e", j=4, p=128)
                        [tci, :, :, et * 128:(et + 1) * 128].rearrange("j p e -> p j e"),
                        ob)
            barrier()
        swait("gpsimd", "d_out0")
        swait("gpsimd", "d_out1")

    emit_phaseA()
    if "B" in phases:
        emit_phaseB()
    if "C" in phases:
        emit_phaseCD()
    else:
        swait("gpsimd", "pro_c")

    block = es.enter_context(nc.Block())

    @block.sync
    def _(e):
        for f in prog["sync"]:
            f(e)

    @block.gpsimd
    def _(e):
        for f in prog["gpsimd"]:
            f(e)

    @block.tensor
    def _(e):
        for f in prog["tensor"]:
            f(e)

    @block.vector
    def _(e):
        for f in prog["vector"]:
            f(e)

    @block.scalar
    def _(e):
        for f in prog["scalar"]:
            f(e)

    es.close()
    return nc


def make_inputs(core, full):
    E_ = 4096
    r = core % 2
    b = core // 2
    x = np.ascontiguousarray(np.asarray(full["L0_attn_hidden_states"])[b])
    wqkv = np.asarray(full["L0_attn_qkv_w"])
    qkvb = np.asarray(full["L0_attn_qkv_b"])
    hs = slice(16 * r * 128, 16 * (r + 1) * 128)
    wq = wqkv[:, 0:4096][:, hs].reshape(E_, 16, 128)
    wk = wqkv[:, 4096:8192][:, hs].reshape(E_, 16, 128)
    wqk = np.ascontiguousarray(np.concatenate([wq, wk], axis=1).transpose(1, 0, 2))
    wv = np.ascontiguousarray(wqkv[:, 8192:12288][:, hs])
    bq = qkvb[0:4096][hs].reshape(16, 128)
    bk = qkvb[4096:8192][hs].reshape(16, 128)
    bv = qkvb[8192:12288][hs].reshape(16, 128)
    bqk = np.ascontiguousarray(np.concatenate([bq, bk], axis=0).T)
    wo = np.asarray(full["L0_attn_out_w"])
    wo_r = np.ascontiguousarray(wo.reshape(32, 128, 32, 128).transpose(2, 0, 1, 3))
    bo = np.ascontiguousarray(np.asarray(full["L0_attn_out_b"]).reshape(32, 128).T)
    mask = np.zeros((128, 1024), np.float32)
    for i in range(128):
        mask[i, 512 + i:] = 1.0
    oh32 = np.zeros((128, NG_H * NG_H), np.float32)
    for g in range(NG_H):
        oh32[:, g * NG_H + g] = 1.0
    oh86 = np.zeros((128, NG_I * NG_I), np.float32)
    for g in range(NG_I):
        oh86[:, g * NG_I + g] = 1.0
    rsel = np.zeros((128, 2), np.float32)
    rsel[:, 0] = 1.0 - r
    rsel[:, 1] = r
    return {
        "x": x,
        "xown": np.ascontiguousarray(x[r * 2048:(r + 1) * 2048]),
        "wqk": wqk, "wv": wv, "bqk": bqk,
        "bv": np.ascontiguousarray(bv.T),
        "wo": wo_r, "bo": bo,
        "gq": np.asarray(full["L0_ffn_gate_qweight"]),
        "gsc": np.asarray(full["L0_ffn_gate_scales"]),
        "gz": np.asarray(full["L0_ffn_gate_zeros"]),
        "uq": np.asarray(full["L0_ffn_up_qweight"]),
        "usc": np.asarray(full["L0_ffn_up_scales"]),
        "uz": np.asarray(full["L0_ffn_up_zeros"]),
        "dq": np.asarray(full["L0_ffn_down_qweight"]),
        "dsc": np.asarray(full["L0_ffn_down_scales"]),
        "dz": np.asarray(full["L0_ffn_down_zeros"]),
        "eye": np.eye(128, dtype=np.float32),
        "maskbig": mask,
        "onescol": np.ones((128, 1), np.float32),
        "onesrow": np.ones((1, 128), np.float32),
        "shiftvec": (4 * (np.arange(128) % 8)).astype(np.int32).reshape(128, 1),
        "iota64": np.arange(64, dtype=np.float32).reshape(1, 64),
        "pos": np.arange(4096, dtype=np.float32).reshape(1, 4096),
        "oh86": oh86, "oh32": oh32, "rsel": rsel,
    }


def kernel(**inputs):
    nc = build(debug=0)
    in_maps = [make_inputs(c, inputs) for c in range(8)]
    res = run_bass_kernel_spmd(nc, in_maps, core_ids=list(range(8)))
    out = np.empty((4, S, E), np.float32)
    for c in range(8):
        b, r = c // 2, c % 2
        out[b, r * 2048:(r + 1) * 2048, :] = np.asarray(res.results[c]["out"])
    return out
